# version 30
# baseline (speedup 1.0000x reference)
import contextlib
import numpy as np

import concourse.bass as bass
import concourse.mybir as mybir
from concourse.bass_utils import run_bass_kernel_spmd

F32 = mybir.dt.float32
BF16 = mybir.dt.bfloat16
AF = mybir.ActivationFunctionType
ALU = mybir.AluOpType
AX = mybir.AxisListType


class Cfg:
    def __init__(self, H=1024, N=16, I=2816, ER=7, T=2048, BPC=2, mini=False):
        self.H, self.N, self.I, self.ER, self.T, self.BPC = H, N, I, ER, T, BPC
        self.E = ER + 1
        self.G = H // 128
        self.KI = I // 128
        assert self.KI % 2 == 0 or True
        self.IH = (self.KI + 1) // 2
        self.IH2 = self.KI - self.IH
        self.UT = min(1024, T)
        self.NT = self.UT // 128
        self.UPB = T // self.UT
        self.UNITS = BPC * self.UPB
        self.CH = min(512, self.UT)
        self.NCH = self.UT // self.CH
        self.mini = mini



BODIES = ("tensor", "scalar", "vector", "gpsimd", "sync")


class Prog:
    def __init__(self, nc, stack):
        self.nc = nc
        self.streams = {}
        self.body_of = {}
        self.cnt = {}
        names = [("pe", "tensor"), ("act", "scalar"), ("dve", "vector"),
                 ("pool", "gpsimd")]
        names += [(f"sd{i}", "sync") for i in range(self.NSD)]
        names += [(f"gd{i}", "gpsimd") for i in range(self.NGD)]
        for s, body in names:
            self.streams[s] = stack.enter_context(nc.semaphore(f"sem_{s}"))
            self.body_of[s] = body
            self.cnt[s] = 0
        self.vc = {b: {} for b in BODIES}
        self.snap = {}
        self.last_write = {}
        self.readers = {}
        self.body_ops = {b: [] for b in BODIES}
        self._sd_rr = 0
        self._gd_rr = 0

    NSD = 5
    NGD = 6
    DMA_STREAMS = frozenset({f"sd{i}" for i in range(NSD)} |
                            {f"gd{i}" for i in range(NGD)})

    def sd_queue(self):
        self._sd_rr += 1
        return "sd%d" % (self._sd_rr % self.NSD)

    def gd_queue(self):
        self._gd_rr += 1
        return "gd%d" % (self._gd_rr % self.NGD)

    def op(self, stream, reads, writes, emit, inc=None):
        body = self.body_of[stream]
        if inc is None:
            inc = 16 if stream in self.DMA_STREAMS else 1
        if stream in self.DMA_STREAMS:
            reads = list(reads) + [("_dmaq", stream)]
            writes = list(writes) + [("_dmaq", stream)]
        needs = {}
        for k in reads:
            lw = self.last_write.get(k)
            if lw is not None:
                s2, v2 = lw
                if needs.get(s2, 0) < v2:
                    needs[s2] = v2
        for k in writes:
            lw = self.last_write.get(k)
            if lw is not None:
                s2, v2 = lw
                if needs.get(s2, 0) < v2:
                    needs[s2] = v2
            for (s2, v2) in self.readers.get(k, ()):
                if needs.get(s2, 0) < v2:
                    needs[s2] = v2
        vc = self.vc[body]
        if stream == "pe" and "pe" in needs:
            del needs["pe"]
        waits = []
        for s2 in sorted(needs):
            v2 = needs[s2]
            if vc.get(s2, 0) < v2:
                waits.append((s2, v2))
                vc[s2] = v2
                sn = self.snap.get((s2, v2))
                if sn:
                    for s3, v3 in sn.items():
                        if vc.get(s3, 0) < v3:
                            vc[s3] = v3
        self.cnt[stream] += inc
        val = self.cnt[stream]
        self.snap[(stream, val)] = dict(vc, **{stream: val})
        for k in writes:
            self.last_write[k] = (stream, val)
            self.readers[k] = []
        for k in reads:
            self.readers.setdefault(k, []).append((stream, val))
        self.body_ops[body].append((waits, emit, stream, inc))

    def emit_all(self, block, final_stores):
        sems = self.streams
        body_ops = self.body_ops

        def run(eng, body):
            for waits, emit, stream, inc in body_ops[body]:
                for s2, v2 in waits:
                    eng.wait_ge(sems[s2], v2)
                inst = emit(eng)
                if inst is not None:
                    inst.then_inc(sems[stream], inc)
            if body == "sync":
                for s2, v2 in final_stores:
                    eng.wait_ge(sems[s2], v2)

        @block.tensor
        def _(eng):
            run(eng, "tensor")

        @block.scalar
        def _(eng):
            run(eng, "scalar")

        @block.vector
        def _(eng):
            run(eng, "vector")

        @block.gpsimd
        def _(eng):
            run(eng, "gpsimd")

        @block.sync
        def _(eng):
            run(eng, "sync")



def build_nc(cfg: Cfg):
    nc = bass.Bass("TRN2", target_bir_lowering=False, debug=False)
    H, N, I, E, G, KI = cfg.H, cfg.N, cfg.I, cfg.E, cfg.G, cfg.KI
    UT, NT, CH, NCH, UNITS = cfg.UT, cfg.NT, cfg.CH, cfg.NCH, cfg.UNITS
    TOK = cfg.BPC * cfg.T

    x_hbm = nc.dram_tensor("x", [TOK, H], F32, kind="ExternalInput").ap()
    out_hbm = nc.dram_tensor("out", [TOK, H], F32, kind="ExternalOutput").ap()
    wdt_hbm = nc.dram_tensor("wdtT", [128, G * H], F32, kind="ExternalInput").ap()
    bdt_hbm = nc.dram_tensor("bdt", [128, G], F32, kind="ExternalInput").ap()
    alog_hbm = nc.dram_tensor("alog", [128, G * N], F32, kind="ExternalInput").ap()
    bmat_hbm = nc.dram_tensor("bmat", [128, G * N], F32, kind="ExternalInput").ap()
    cmat_hbm = nc.dram_tensor("cmat", [128, G * N], F32, kind="ExternalInput").ap()
    dvec_hbm = nc.dram_tensor("dvec", [128, G], F32, kind="ExternalInput").ap()
    wr_hbm = nc.dram_tensor("wrT", [128, G * (E - 1)], F32,
                            kind="ExternalInput").ap()
    rb_hbm = nc.dram_tensor("rb", [E - 1], F32, kind="ExternalInput").ap()
    wgu_hbm, d_hbm = [], []
    for e in range(E):
        wgu_hbm.append(nc.dram_tensor(f"wgu{e}", [KI * 128, G * 256], F32,
                                      kind="ExternalInput").ap())
        d_hbm.append(nc.dram_tensor(f"dT{e}", [I * H], F32,
                                    kind="ExternalInput").ap())
    wgu_bf, d_bf = [], []
    for e in range(E):
        wgu_bf.append(nc.dram_tensor(f"wgubf{e}", [KI * 128, G * 256], BF16,
                                     kind="Internal").ap())
        d_bf.append(nc.dram_tensor(f"dbf{e}", [I * H], BF16,
                                   kind="Internal").ap())

    stack = contextlib.ExitStack()
    sb = lambda name, shape, dt: stack.enter_context(nc.sbuf_tensor(name, shape, dt))
    psb = lambda name, shape, dt: stack.enter_context(nc.psum_tensor(name, shape, dt))

    xnT = sb("xnT", [128, G, UT], BF16)
    act = sb("act", [128, cfg.IH, UT], BF16)
    acc = sb("acc", [128, NT, H], F32)
    ytok = sb("ytok", [128, NT, H], BF16)
    dtbf = sb("dtbf", [128, G, UT], BF16)
    wdtT = sb("wdtT_sb", [128, G, H], BF16)
    du = [sb(f"du{i}", [128, UT], F32) for i in range(2)]
    NAB = 3
    abuf = [sb(f"abuf{i}", [128, CH], F32) for i in range(NAB)]
    bbuf = [sb(f"bbuf{i}", [128, CH], F32) for i in range(2)]
    sbuf_s = [sb(f"sbuf{i}", [128, CH], F32) for i in range(2)]
    ysl = [sb(f"ysl{i}", [128, UT], F32) for i in range(2)]
    tsb = [sb(f"tsb{i}", [128, CH], F32) for i in range(2)]
    WGB = min(128, I)
    NWB = 3
    wgu = [sb(f"wgub{i}", [128, G, 256], BF16) for i in range(NWB)]
    WDB = min(4, KI)
    NWD = 4
    wd = [sb(f"wd{i}", [128, WDB, 512 if H >= 1024 else H], BF16)
          for i in range(NWD)]
    xns = [sb(f"xns{i}", [128, H], BF16) for i in range(2)]
    xres = [sb(f"xres{i}", [128, min(512, H)], F32) for i in range(2)]
    xb16 = sb("xb16", [128, NT, H], BF16)
    stats = sb("stats", [128, 64], F32)
    rt = sb("rt", [128, 4 * (E - 1)], F32)
    rsc = sb("rsc", [128, 16], F32)
    wtok = sb("wtok", [128, NT, E - 1], F32)
    a_neg = sb("a_neg", [128, G, N], F32)
    cb = sb("cb", [128, G, N], F32)
    dv = sb("dv", [128, G], F32)
    bdt_sb = sb("bdt_sb", [128, G], F32)
    carry = sb("carry", [128, G, N], F32)
    wrT_sb = sb("wrT_sb", [128, G, E - 1], BF16)
    rb_row = sb("rb_row", [1, E - 1], F32)
    ones_row = sb("ones_row", [1, 128], F32)
    ident_bf = sb("ident_bf", [128, 128], BF16)
    ident_f = sb("ident_f", [128, 128], F32)
    ones_col = sb("ones_col", [128, 1], F32)

    ps_gate = [psb(f"ps_gate{i}", [128, CH], F32) for i in range(2)]
    ps_up = [psb(f"ps_up{i}", [128, CH], F32) for i in range(2)]
    ps_dn = [psb(f"ps_dn{i}", [128, min(512, H)], F32) for i in range(2)]
    GROT = [ps_gate[0], ps_gate[1]]
    GROTK = [("ps_gate", 0), ("ps_gate", 1)]
    UROT = [ps_up[0], ps_up[1]]
    UROTK = [("ps_up", 0), ("ps_up", 1)]
    DROT = [ps_dn[0], ps_dn[1], ps_up[0], ps_up[1], ps_gate[0], ps_gate[1]]
    DROTK = [("ps_dn", 0), ("ps_dn", 1), ("ps_up", 0), ("ps_up", 1),
             ("ps_gate", 0), ("ps_gate", 1)]
    ps_m32 = psb("ps_m32", [128, 512], F32)
    ps_mbf = psb("ps_mbf", [128, 1024], BF16)

    p = Prog(nc, stack)
    K_CONST = ("const",)

    def dma(stream_pick, dst_ap, src_ap, reads, writes, slow=False):
        q = stream_pick()
        if slow:
            p.op(q, reads, writes,
                 (lambda e, d=dst_ap, s=src_ap: e.dma_start(
                     out=d, in_=s, allow_slow_non_contiguous=True)))
        else:
            p.op(q, reads, writes,
                 (lambda e, d=dst_ap, s=src_ap: e.dma_start(out=d, in_=s)))

    p.op("pool", [], [K_CONST + ("idb",)], lambda e: e.memset(ident_bf[:], 0.0))
    p.op("pool", [K_CONST + ("idb",)], [K_CONST + ("idb",)],
         lambda e: e.affine_select(out=ident_bf[:], in_=ident_bf[:],
                                   compare_op=ALU.not_equal, fill=1.0, base=0,
                                   pattern=[[-1, 128]], channel_multiplier=1))
    p.op("pool", [], [K_CONST + ("idf",)], lambda e: e.memset(ident_f[:], 0.0))
    p.op("pool", [K_CONST + ("idf",)], [K_CONST + ("idf",)],
         lambda e: e.affine_select(out=ident_f[:], in_=ident_f[:],
                                   compare_op=ALU.not_equal, fill=1.0, base=0,
                                   pattern=[[-1, 128]], channel_multiplier=1))
    p.op("pool", [], [K_CONST + ("ones",)], lambda e: e.memset(ones_col[:], 1.0))

    dma(p.gd_queue, wdtT[:],
        wdt_hbm.rearrange("p (kk g) -> p kk g", kk=G),
        [], [("wdtT",)])
    dma(p.gd_queue, a_neg[:], alog_hbm.rearrange("p (g n) -> p g n", g=G),
        [], [K_CONST + ("alog",)])
    dma(p.gd_queue, cb[:], bmat_hbm.rearrange("p (g n) -> p g n", g=G),
        [], [K_CONST + ("cbB",)])
    dma(p.gd_queue, ysl[0][:, 0:G * N].rearrange("p (g n) -> p g n", g=G),
        cmat_hbm.rearrange("p (g n) -> p g n", g=G),
        [], [("ysl", 0)])
    dma(p.gd_queue, dv[:], dvec_hbm, [], [K_CONST + ("dv",)])
    dma(p.gd_queue, bdt_sb[:], bdt_hbm, [], [K_CONST + ("bdt",)])
    dma(p.gd_queue, wrT_sb[:], wr_hbm.rearrange("p (kk e) -> p kk e", kk=G),
        [], [K_CONST + ("wrT",)])
    dma(p.gd_queue, rb_row[:], rb_hbm[None, :], [], [K_CONST + ("rb",)])
    p.op("pool", [], [K_CONST + ("ones_row",)],
         lambda e: e.memset(ones_row[:], 1.0))
    p.op("act", [K_CONST + ("alog",)], [K_CONST + ("alog",)],
         lambda e: e.activation(out=a_neg[:], in_=a_neg[:], func=AF.Exp))
    p.op("dve", [K_CONST + ("alog",)], [K_CONST + ("A",)],
         lambda e: e.tensor_scalar(out=a_neg[:], in0=a_neg[:], scalar1=-1.0,
                                   scalar2=None, op0=ALU.mult))
    p.op("dve", [K_CONST + ("cbB",), ("ysl", 0)], [K_CONST + ("cb",)],
         lambda e: e.tensor_tensor(
             out=cb[:], in0=cb[:],
             in1=ysl[0][:, 0:G * N].rearrange("p (g n) -> p g n", g=G),
             op=ALU.mult))

    def ln_steps_a(u, i, xrow0):
        bi = i % 2
        kx = ("xb16", i)
        kst = ("st", i)
        so = 8 * i
        st = []
        st.append(lambda: dma(p.gd_queue, xb16[:, i, :],
                              x_hbm[xrow0:xrow0 + 128, :], [], [kx]))
        st.append(lambda: p.op("act", [kx], [("xns", bi), kst],
             lambda e: e.activation(
                 out=xns[bi][:], in_=xb16[:, i, :], func=AF.Identity,
                 accum_out=stats[:, so + 0:so + 1])))
        st.append(lambda: p.op("act", [kst], [kst],
             lambda e: e.mul(stats[:, so + 1:so + 2],
                             stats[:, so + 0:so + 1], -1.0 / H)))
        st.append(lambda: p.op("act", [kx, kst], [("xns", bi), kst],
             lambda e: e.activation(
                 out=xns[bi][:], in_=xb16[:, i, :], func=AF.Square,
                 bias=stats[:, so + 1:so + 2],
                 accum_out=stats[:, so + 2:so + 3])))
        st.append(lambda: p.op("dve", [kst], [kst],
             lambda e: e.tensor_scalar(
                 out=stats[:, so + 3:so + 4], in0=stats[:, so + 2:so + 3],
                 scalar1=1.0 / H, scalar2=1e-5, op0=ALU.mult, op1=ALU.add)))
        st.append(lambda: p.op("act", [kst], [kst],
             lambda e: e.activation(
                 out=stats[:, so + 4:so + 5], in_=stats[:, so + 3:so + 4],
                 func=AF.Ln)))
        st.append(lambda: p.op("act", [kst], [kst],
             lambda e: e.activation(
                 out=stats[:, so + 5:so + 6], in_=stats[:, so + 4:so + 5],
                 func=AF.Exp, scale=-0.5)))
        st.append(lambda: p.op("dve", [kst], [kst],
             lambda e: e.tensor_tensor(
                 out=stats[:, so + 6:so + 7], in0=stats[:, so + 1:so + 2],
                 in1=stats[:, so + 5:so + 6], op=ALU.mult)))
        return st

    def ln_steps_b(u, i, xrow0):
        bi = i % 2
        kst = ("st", i)
        so = 8 * i
        st = []
        st.append(lambda: p.op("act", [("xb16", i), kst], [("xns", bi)],
             lambda e: e.activation(
                 out=xns[bi][:], in_=xb16[:, i, :], func=AF.Identity,
                 scale=stats[:, so + 5:so + 6], bias=stats[:, so + 6:so + 7])))

        def tr(e):
            last = None
            for j in range(G):
                last = e.transpose(ps_mbf[:, 128 * j:128 * (j + 1)],
                                   xns[bi][:, 128 * j:128 * (j + 1)],
                                   ident_bf[:])
            return last
        st.append(lambda: p.op("pe", [("xns", bi), K_CONST + ("idb",)],
                               [("ps_mbf",)], tr))
        st.append(lambda: p.op("dve", [("ps_mbf",)], [("xnT", i)],
             lambda e: e.tensor_copy(
                 out=xnT[:, :, 128 * i:128 * (i + 1)],
                 in_=ps_mbf[:, 0:128 * G].rearrange("p (g c) -> p g c", g=G))))
        return st

    def router_steps(u, i):
        bi = i % 2
        ER = E - 1
        ko = ("rt", bi)
        krs = ("rsc", bi)
        l_sl = rt[:, bi * 2 * ER: bi * 2 * ER + ER]
        e_sl = rt[:, bi * 2 * ER + ER: bi * 2 * ER + 2 * ER]
        r0 = bi * 8
        st = []

        def mm(e):
            for kk in range(G):
                e.matmul(ps_m32[:, 0:ER],
                         xnT[:, kk, 128 * i:128 * (i + 1)],
                         wrT_sb[:, kk, :],
                         start=(kk == 0), stop=False)
            return e.matmul(ps_m32[:, 0:ER], ones_row[:], rb_row[:],
                            start=False, stop=True)
        st.append(lambda: p.op("pe", [("xnT", i), K_CONST + ("wrT",),
                                      K_CONST + ("rb",),
                                      K_CONST + ("ones_row",)],
                               [("ps_m32",)], mm))
        st.append(lambda: p.op("dve", [("ps_m32",)], [ko],
             lambda e: e.tensor_copy(out=l_sl, in_=ps_m32[:, 0:E - 1])))
        st.append(lambda: p.op("dve", [ko], [krs],
             lambda e: e.tensor_reduce(
                 out=rsc[:, r0:r0 + 1], in_=l_sl, axis=AX.X, op=ALU.max)))
        st.append(lambda: p.op("dve", [ko, krs], [ko],
             lambda e: e.tensor_scalar(
                 out=e_sl, in0=l_sl, scalar1=rsc[:, r0:r0 + 1], scalar2=None,
                 op0=ALU.subtract)))
        st.append(lambda: p.op("act", [ko], [ko],
             lambda e: e.activation(out=e_sl, in_=e_sl, func=AF.Exp)))
        st.append(lambda: p.op("dve", [ko], [ko],
             lambda e: e.tensor_scalar(
                 out=l_sl, in0=e_sl, scalar1=1.0, scalar2=None,
                 op0=ALU.is_lt)))
        st.append(lambda: p.op("dve", [ko], [ko],
             lambda e: e.tensor_tensor(
                 out=l_sl, in0=l_sl, in1=e_sl, op=ALU.mult)))
        st.append(lambda: p.op("dve", [ko], [krs],
             lambda e: e.tensor_reduce(
                 out=rsc[:, r0 + 1:r0 + 2], in_=l_sl, axis=AX.X, op=ALU.max)))
        st.append(lambda: p.op("dve", [krs], [krs],
             lambda e: e.tensor_scalar(
                 out=rsc[:, r0 + 2:r0 + 3], in0=rsc[:, r0 + 1:r0 + 2],
                 scalar1=1.0, scalar2=None, op0=ALU.add)))
        st.append(lambda: p.op("dve", [krs], [krs],
             lambda e: e.reciprocal(out=rsc[:, r0 + 3:r0 + 4],
                                    in_=rsc[:, r0 + 2:r0 + 3])))
        st.append(lambda: p.op("dve", [ko, krs], [("wtok", i)],
             lambda e: e.tensor_scalar(
                 out=wtok[:, i, :], in0=e_sl, scalar1=rsc[:, r0 + 1:r0 + 2],
                 scalar2=None, op0=ALU.is_ge)))
        st.append(lambda: p.op("dve", [ko, ("wtok", i)], [("wtok", i)],
             lambda e: e.tensor_tensor(
                 out=wtok[:, i, :], in0=wtok[:, i, :], in1=e_sl,
                 op=ALU.mult)))
        st.append(lambda: p.op("dve", [krs, ("wtok", i)], [("wtok", i)],
             lambda e: e.tensor_scalar(
                 out=wtok[:, i, :], in0=wtok[:, i, :],
                 scalar1=rsc[:, r0 + 3:r0 + 4], scalar2=None, op0=ALU.mult)))
        return st

    def dt_slab(u, g):
        for c in range(NCH):
            xnT_keys = [("xnT", i) for i in range(NT)
                        if CH * c <= 128 * i < CH * (c + 1)]
            pk = ("ps_dn", c % 2)

            def mm(e, g=g, c=c, pk=pk):
                last = None
                for kk in range(G):
                    last = e.matmul(ps_dn[c % 2][:, 0:CH],
                                    wdtT[:, kk, 128 * g:128 * (g + 1)],
                                    xnT[:, kk, CH * c:CH * (c + 1)],
                                    start=(kk == 0), stop=(kk == G - 1))
                return last
            p.op("pe", xnT_keys + [("wdtT",)], [pk], mm)
            ka = ("a", (2 * g + c) % NAB)
            p.op("act", [pk, K_CONST + ("bdt",)], [ka],
                 lambda e, g=g, c=c, ka=ka: e.activation(
                     out=abuf[ka[1]][:], in_=ps_dn[c % 2][:, 0:CH], func=AF.Exp,
                     bias=bdt_sb[:, g:g + 1]))
            p.op("act", [ka, K_CONST + ("ones",)], [("dtbf", g, c)],
                 lambda e, g=g, c=c, ka=ka: e.activation(
                     out=dtbf[:, g, CH * c:CH * (c + 1)], in_=abuf[ka[1]][:],
                     func=AF.Ln, bias=ones_col[:]))

    def scan_slab_thunks(u, g, first_unit_of_batch):
        gb = g % 2
        kdu = ("du", gb)
        kysl = ("ysl", gb)
        thunks = []

        def t_init():
            p.op("dve", [("dtbf", g, 0), ("dtbf", g, min(1, NCH - 1))] +
                 [("xnT", i) for i in range(NT)], [kdu],
                 lambda e, g=g, gb=gb: e.tensor_tensor(
                     out=du[gb][:], in0=dtbf[:, g, :], in1=xnT[:, g, :],
                     op=ALU.mult))
            for c in range(NCH):
                p.op("act", [("xnT", i) for i in range(NT)] +
                     [K_CONST + ("dv",)], [kysl],
                     lambda e, g=g, c=c, gb=gb: e.activation(
                         out=ysl[gb][:, CH * c:CH * (c + 1)],
                         in_=xnT[:, g, CH * c:CH * (c + 1)], func=AF.Identity,
                         scale=dv[:, g:g + 1]))
            if first_unit_of_batch:
                p.op("dve", [], [("carry", g)],
                     lambda e, g=g: e.memset(carry[:, g, :], 0.0))
        thunks.append(t_init)

        def t_iter(n, c):
            ia = (n * NCH + c) % NAB
            ka = ("a", ia)
            p.op("act", [("dtbf", g, c), K_CONST + ("A",)], [ka],
                 lambda e, g=g, c=c, n=n, ia=ia: e.activation(
                     out=abuf[ia][:], in_=dtbf[:, g, CH * c:CH * (c + 1)],
                     func=AF.Exp, scale=a_neg[:, g, n:n + 1]))
            ib2 = (n * NCH + c) % 2
            kb = ("b", ib2)
            p.op("act", [kdu, K_CONST + ("cb",)], [kb],
                 lambda e, ib2=ib2, g=g, n=n, c=c, gb=gb: e.activation(
                     out=bbuf[ib2][:], in_=du[gb][:, CH * c:CH * (c + 1)],
                     func=AF.Copy, scale=cb[:, g, n:n + 1]))
            isb = (n * NCH + c) % 2
            ks = ("s", isb)
            if c == 0:
                init_ap = carry[:, g, n:n + 1]
                init_key = ("carry", g)
            else:
                prev = (n * NCH + c - 1) % 2
                init_ap = sbuf_s[prev][:, CH - 1:CH]
                init_key = ("s", prev)
            p.op("dve", [ka, kb, init_key], [ks],
                 lambda e, ia=ia, ib2=ib2, isb=isb, init_ap=init_ap:
                 e.tensor_tensor_scan(
                     out=sbuf_s[isb][:], data0=abuf[ia][:],
                     data1=bbuf[ib2][:],
                     initial=init_ap, op0=ALU.mult, op1=ALU.add))
            p.op("dve", [ks, kysl], [kysl],
                 lambda e, isb=isb, c=c, gb=gb: e.tensor_tensor(
                     out=ysl[gb][:, CH * c:CH * (c + 1)],
                     in0=ysl[gb][:, CH * c:CH * (c + 1)],
                     in1=sbuf_s[isb][:], op=ALU.add))
            if c == NCH - 1:
                p.op("dve", [ks], [("carry", g)],
                     lambda e, isb=isb, g=g, n=n: e.tensor_copy(
                         out=carry[:, g, n:n + 1],
                         in_=sbuf_s[isb][:, CH - 1:CH]))
        for n in range(N):
            for c in range(NCH):
                thunks.append(lambda n=n, c=c: t_iter(n, c))

        def t_tr(j0, j1):
            def tr(e, j0=j0, j1=j1, gb=gb):
                last = None
                for j in range(j0, j1):
                    last = e.transpose(
                        ps_m32[:, 128 * (j - j0):128 * (j - j0 + 1)],
                        ysl[gb][:, 128 * j:128 * (j + 1)], ident_f[:])
                return last
            p.op("pe", [kysl, K_CONST + ("idf",)], [("ps_m32",)], tr)
            p.op("dve", [("ps_m32",)],
                 [("ytok", j, g) for j in range(j0, j1)],
                 lambda e, j0=j0, j1=j1, g=g: e.tensor_copy(
                     out=ytok[:, j0:j1, 128 * g:128 * (g + 1)],
                     in_=ps_m32[:, 0:128 * (j1 - j0)].rearrange(
                         "p (a b) -> p a b", a=j1 - j0)))
        nblk = UT // 128
        tr_thunks = []
        for half in range((nblk + 3) // 4):
            j0, j1 = 4 * half, min(4 * half + 4, nblk)
            tr_thunks.append(lambda j0=j0, j1=j1: t_tr(j0, j1))
        return thunks, tr_thunks

    def gateup_blocks(it0, it1):
        blocks = []
        col = it0 * 128
        while col < it1 * 128:
            bw = min(WGB, it1 * 128 - col)
            blocks.append((col, bw))
            col += bw
        return blocks

    class WeightWheel:

        def __init__(self, items, ahead=3):
            self.items = items
            self.ahead = ahead
            self.emitted = 0
            self.consumed = 0

        def _pump(self, target):
            while self.emitted < min(len(self.items), target):
                u_i, e_i, col, bw = self.items[self.emitted]
                seq = self.emitted
                b = col // 128
                if u_i == 0:
                    dma(p.gd_queue, wgu[seq % NWB][:],
                        wgu_hbm[e_i][128 * b:128 * (b + 1), :]
                        .rearrange("p (kk c) -> p kk c", kk=G),
                        [], [("wgu", seq % NWB)])
                    dma(p.sd_queue,
                        wgu_bf[e_i][128 * b:128 * (b + 1), :]
                        .rearrange("p (kk c) -> p kk c", kk=G),
                        wgu[seq % NWB][:],
                        [("wgu", seq % NWB)], [("gbf", e_i, b)])
                else:
                    dma(p.sd_queue, wgu[seq % NWB][:],
                        wgu_bf[e_i][128 * b:128 * (b + 1), :]
                        .rearrange("p (kk c) -> p kk c", kk=G),
                        [("gbf", e_i, b)], [("wgu", seq % NWB)])
                self.emitted += 1

        def ensure_ahead(self):
            s = self.consumed
            self.consumed = s + 1
            self._pump(s + self.ahead)
            return s

        def prefetch(self):
            self._pump(self.consumed + self.ahead)

    def _wd_offsets():
        HH0 = min(512, H)
        off = {}
        pos = 0
        for hidx, (it0, it1) in enumerate([(0, cfg.IH), (cfg.IH, KI)]):
            for hh in range(H // HH0):
                kt_ = it0
                bidx = 0
                while kt_ < it1:
                    bk = min(WDB, it1 - kt_)
                    off[(hidx, hh, bidx)] = (pos, kt_, bk)
                    pos += 128 * bk * HH0
                    kt_ += bk
                    bidx += 1
        return off
    WD_OFF = _wd_offsets()

    def ffn_expert(u, e_idx, drain, wheel):
        dhbm = d_hbm[e_idx]
        xnT_chunk_keys = [[("xnT", i) for i in range(NT)
                           if CH * c <= 128 * i < CH * (c + 1)]
                          for c in range(NCH)]
        halves = [(0, cfg.IH), (cfg.IH, KI)]
        for hidx, (it0, it1) in enumerate(halves):
            if it1 <= it0:
                continue
            ntile_half = it1 - it0
            HH0 = min(512, H)
            dblocks_by_hh = {}
            for hh in range(H // HH0):
                dblocks = []
                kt_ = it0
                while kt_ < it1:
                    bk = min(WDB, it1 - kt_)
                    dblocks.append((kt_, bk))
                    kt_ += bk
                dblocks_by_hh[hh] = dblocks

            nblk_hh = len(dblocks_by_hh[0])

            def emit_wd_dmas(hh, only=None):
                for bidx, (kt0, bk) in enumerate(dblocks_by_hh[hh]):
                    wdi = (nblk_hh * hh + bidx) % NWD
                    if only is not None and wdi not in only:
                        continue
                    pos, _kt0, _bk = WD_OFF[(hidx, hh, bidx)]
                    kbf = ("dbf", e_idx, hidx, hh, bidx)
                    if u == 0:
                        dma(p.gd_queue, wd[wdi][:, 0:bk, 0:HH0],
                            dhbm[pos:pos + 128 * bk * HH0]
                            .rearrange("(p kk h) -> p kk h", p=128, kk=bk),
                            [], [("wd", wdi)])
                        dma(p.sd_queue,
                            d_bf[e_idx][pos:pos + 128 * bk * HH0]
                            .rearrange("(p kk h) -> p kk h", p=128, kk=bk),
                            wd[wdi][:, 0:bk, 0:HH0],
                            [("wd", wdi)], [kbf])
                    else:
                        dma(p.sd_queue, wd[wdi][:, 0:bk, 0:HH0],
                            d_bf[e_idx][pos:pos + 128 * bk * HH0]
                            .rearrange("(p kk h) -> p kk h", p=128, kk=bk),
                            [kbf], [("wd", wdi)])
            emit_wd_dmas(0)
            blocks = gateup_blocks(it0, it1)
            for bidx, (col, bw) in enumerate(blocks):
                seq = wheel.ensure_ahead()
                kwg = ("wgu", seq % NWB)
                kwu = kwg
                for sub in range(bw // 128):
                    it = (col + 128 * sub) // 128
                    lt = it - it0
                    for c in range(NCH):
                        ri = (it * NCH + c) % len(GROT)
                        pg = GROTK[ri]
                        pu = UROTK[ri]

                        def mmg(e, sub=sub, c=c, ri=ri, seq=seq):
                            last = None
                            for kk in range(G):
                                last = e.matmul(
                                    GROT[ri][:],
                                    wgu[seq % NWB][:, kk, 0:128],
                                    xnT[:, kk, CH * c:CH * (c + 1)],
                                    start=(kk == 0), stop=(kk == G - 1))
                            return last
                        p.op("pe", [kwg] + xnT_chunk_keys[c], [pg], mmg)

                        def mmu(e, sub=sub, c=c, ri=ri, seq=seq):
                            last = None
                            for kk in range(G):
                                last = e.matmul(
                                    UROT[ri][:],
                                    wgu[seq % NWB][:, kk, 128:256],
                                    xnT[:, kk, CH * c:CH * (c + 1)],
                                    start=(kk == 0), stop=(kk == G - 1))
                            return last
                        p.op("pe", [kwu] + xnT_chunk_keys[c], [pu], mmu)
                        kt = ("tsb", (it * NCH + c) % 2)
                        p.op("act", [pg], [kt],
                             lambda e, it=it, c=c, ri=ri: e.activation(
                                 out=tsb[(it * NCH + c) % 2][:],
                                 in_=GROT[ri][:],
                                 func=AF.Tanh, scale=0.5))
                        p.op("dve", [kt, pg], [kt],
                             lambda e, it=it, c=c, ri=ri: e.scalar_tensor_tensor(
                                 out=tsb[(it * NCH + c) % 2][:],
                                 in0=tsb[(it * NCH + c) % 2][:], scalar=1.0,
                                 in1=GROT[ri][:],
                                 op0=ALU.add, op1=ALU.mult))
                        p.op("dve", [kt, pu], [("act", lt, c)],
                             lambda e, it=it, c=c, lt=lt, ri=ri:
                             e.scalar_tensor_tensor(
                                 out=act[:, lt, CH * c:CH * (c + 1)],
                                 in0=tsb[(it * NCH + c) % 2][:], scalar=0.5,
                                 in1=UROT[ri][:],
                                 op0=ALU.mult, op1=ALU.mult))
                        drain(2)
            HH = min(512, H)
            nhh = H // HH
            def finish_tg(tg, hh):
                di = tg % min(4, NT, len(DROT))
                pk = DROTK[di]
                kacc = ("acc", tg, hh)
                if e_idx == 0 and hidx == 0:
                    p.op("dve", [pk], [kacc],
                         lambda e, tg=tg, hh=hh, di=di: e.tensor_copy(
                             out=acc[:, tg, HH * hh:HH * (hh + 1)],
                             in_=DROT[di][:, 0:HH]))
                elif e_idx == 0:
                    p.op("dve", [pk, kacc], [kacc],
                         lambda e, tg=tg, hh=hh, di=di: e.tensor_tensor(
                             out=acc[:, tg, HH * hh:HH * (hh + 1)],
                             in0=acc[:, tg, HH * hh:HH * (hh + 1)],
                             in1=DROT[di][:, 0:HH], op=ALU.add))
                else:
                    p.op("dve", [pk, kacc, ("wtok", tg)], [kacc],
                         lambda e, tg=tg, hh=hh, e_idx=e_idx, di=di:
                         e.scalar_tensor_tensor(
                             out=acc[:, tg, HH * hh:HH * (hh + 1)],
                             in0=DROT[di][:, 0:HH],
                             scalar=wtok[:, tg, e_idx - 1:e_idx],
                             in1=acc[:, tg, HH * hh:HH * (hh + 1)],
                             op0=ALU.mult, op1=ALU.add))

            for hh in range(nhh):
                wheel.prefetch()
                cur_bufs = {(nblk_hh * hh + b) % NWD
                            for b in range(len(dblocks_by_hh[hh]))}
                nxt_bufs = {(nblk_hh * (hh + 1) + b) % NWD
                            for b in range(len(dblocks_by_hh[hh]))}
                if hh + 1 < nhh:
                    emit_wd_dmas(hh + 1, only=nxt_bufs - cur_bufs)
                dblocks = dblocks_by_hh[hh]
                WAVE = min(4, NT, len(DROT))

                def emit_dblock(tg, bidx, di):
                    kt0, bk = dblocks[bidx]
                    pk = DROTK[di]

                    def mmd(e, tg=tg, hh=hh, it0=it0, it1=it1,
                            kt0=kt0, bk=bk, bidx=bidx, di=di):
                        last = None
                        for kk in range(bk):
                            kt_g = kt0 + kk
                            last = e.matmul(
                                DROT[di][:, 0:HH],
                                act[:, kt_g - it0, 128 * tg:128 * (tg + 1)],
                                wd[(nblk_hh * hh + bidx) % NWD][:, kk, 0:HH],
                                start=(kt_g == it0), stop=(kt_g == it1 - 1))
                        return last
                    blk_act_keys = [("act", kt0 - it0 + kk, c)
                                    for kk in range(bk) for c in range(NCH)]
                    p.op("pe", blk_act_keys +
                         [("wd", (nblk_hh * hh + bidx) % NWD), pk], [pk], mmd)

                for w0 in range(0, NT, WAVE):
                    wave = range(w0, min(w0 + WAVE, NT))
                    for tg in wave:
                        for bidx in range(len(dblocks) - 1):
                            emit_dblock(tg, bidx, tg % WAVE)
                    for tg in wave:
                        emit_dblock(tg, len(dblocks) - 1, tg % WAVE)
                        finish_tg(tg, hh)
                        drain(2)
                if hh + 1 < nhh:
                    emit_wd_dmas(hh + 1, only=nxt_bufs & cur_bufs)

    def d_tile_thunks(u, i, xrow0):
        return [lambda: d_tile_calc(u, i, 0, xrow0),
                (lambda: d_tile_calc(u, i, 1, xrow0)) if H > 512 else None]

    def d_tile_calc(u, i, hh, xrow0):
        HH = min(512, H)
        ia = hh % 2
        dma(p.sd_queue, xres[ia][:, 0:HH],
            x_hbm[xrow0:xrow0 + 128, HH * hh:HH * (hh + 1)],
            [], [("xres", ia)])
        ka = ("a", ia)
        p.op("act", [("acc", i, hh)], [ka],
             lambda e, i=i, hh=hh, ia=ia: e.activation(
                 out=abuf[ia][:, 0:HH], in_=acc[:, i, HH * hh:HH * (hh + 1)],
                 func=AF.Tanh, scale=0.5))
        ksv = ("s", ia % 2)
        p.op("dve", [ka, ("acc", i, hh)], [ksv],
             lambda e, i=i, hh=hh, ia=ia: e.scalar_tensor_tensor(
                 out=sbuf_s[ia % 2][:, 0:HH], in0=abuf[ia][:, 0:HH],
                 scalar=1.0, in1=acc[:, i, HH * hh:HH * (hh + 1)],
                 op0=ALU.add, op1=ALU.mult))
        ytk = [("ytok", i, g2) for g2 in range(G)
               if HH * hh <= 128 * g2 < HH * (hh + 1)]
        p.op("dve", [ksv] + ytk, [ksv],
             lambda e, i=i, hh=hh, ia=ia: e.scalar_tensor_tensor(
                 out=sbuf_s[ia % 2][:, 0:HH], in0=sbuf_s[ia % 2][:, 0:HH],
                 scalar=0.5, in1=ytok[:, i, HH * hh:HH * (hh + 1)],
                 op0=ALU.mult, op1=ALU.mult))
        kxr = ("xres", ia % 2)
        p.op("dve", [ksv, kxr], [ksv],
             lambda e, i=i, hh=hh, ia=ia: e.tensor_tensor(
                 out=sbuf_s[ia % 2][:, 0:HH], in0=sbuf_s[ia % 2][:, 0:HH],
                 in1=xres[ia % 2][:, 0:HH], op=ALU.add))
        p.op(p.sd_queue(), [ksv], [("outdma", u, i, hh)],
             lambda e, i=i, hh=hh, ia=ia, xrow0=xrow0: e.dma_start(
                 out=out_hbm[xrow0:xrow0 + 128, HH * hh:HH * (hh + 1)],
                 in_=sbuf_s[ia % 2][:, 0:HH]))

    from collections import deque
    pending = deque()

    witems = []
    for _u in range(UNITS):
        for e_i in range(E):
            for (it0, it1) in [(0, cfg.IH), (cfg.IH, KI)]:
                if it1 > it0:
                    for (col, bw) in gateup_blocks(it0, it1):
                        witems.append((_u, e_i, col, bw))
    wheel = WeightWheel(witems)

    def drain(k):
        for _ in range(k):
            if not pending:
                return
            t = pending.popleft()
            if t is not None:
                t()

    for u in range(UNITS):
        base = u * UT
        first_of_batch = (u % cfg.UPB) == 0
        import itertools
        if u == 0:
            for i in range(NT):
                for t in ln_steps_a(0, i, 128 * i):
                    t()
        for i0 in range(0, NT, 2):
            sa = ln_steps_b(u, i0, base + 128 * i0) + router_steps(u, i0)
            sb = (ln_steps_b(u, i0 + 1, base + 128 * (i0 + 1)) +
                  router_steps(u, i0 + 1)) if i0 + 1 < NT else []
            OFF = 2
            for k in range(len(sa) + OFF):
                if k < len(sa):
                    sa[k]()
                if 0 <= k - OFF < len(sb):
                    sb[k - OFF]()
            drain(2)
        if u > 0:
            for i in range(NT):
                pending.extend(d_tile_thunks(u - 1, i, (u - 1) * UT + 128 * i))
        for g in range(G):
            dt_slab(u, g)
            drain(2)
        tr_pending = None

        def pend_slab(g):
            nonlocal tr_pending
            it_th, tr_th = scan_slab_thunks(u, g, first_of_batch)
            pending.extend(it_th)
            if tr_pending:
                pending.extend(tr_pending)
            tr_pending = tr_th
        pend_slab(0)
        for e_idx in range(E):
            if e_idx == min(4, E - 1) and u + 1 < UNITS:
                for i in range(NT):
                    pending.extend(ln_steps_a(u + 1, i,
                                              (u + 1) * UT + 128 * i))
            if e_idx + 1 < G:
                pend_slab(e_idx + 1)
            ffn_expert(u, e_idx, drain, wheel)
        for g in range(E, G):
            pend_slab(g)
        if tr_pending:
            pending.extend(tr_pending)
        drain(len(pending))
    u = UNITS - 1
    for i in range(NT):
        pending.extend(d_tile_thunks(u, i, u * UT + 128 * i))
    drain(len(pending))

    finals = []
    for s2 in sorted(Prog.DMA_STREAMS):
        if p.cnt[s2]:
            finals.append((s2, p.cnt[s2]))
    with nc.Block() as block:
        p.emit_all(block, finals)
    stack.close()
    return nc



def _prep_weights(cfg, kw):
    c = np.ascontiguousarray
    H, I, G, N, KI = cfg.H, cfg.I, cfg.G, cfg.N, cfg.KI
    ER = cfg.E - 1

    def tile_h(a):
        a = np.asarray(a)
        rest = a.shape[1:]
        out = a.reshape(G, 128, *rest).transpose(1, 0, *range(2, a.ndim + 1))
        return c(out.reshape(128, -1))

    def tile_wgu(g, u):
        gt = g.T.reshape(G, 128, KI, 128)
        ut = u.T.reshape(G, 128, KI, 128)
        st = np.concatenate([gt, ut], axis=-1)
        return c(st.transpose(2, 1, 0, 3).reshape(KI * 128, G * 256))

    def tile_d(dmat):
        dT = dmat.T
        HH0 = min(512, H)
        IH = cfg.IH
        WDB = min(4, KI)
        parts = []
        for (it0, it1) in [(0, IH), (IH, KI)]:
            for hh in range(H // HH0):
                kt_ = it0
                while kt_ < it1:
                    bk = min(WDB, it1 - kt_)
                    blk = dT[128 * kt_:128 * (kt_ + bk),
                             HH0 * hh:HH0 * (hh + 1)]
                    parts.append(blk.reshape(bk, 128, HH0)
                                 .transpose(1, 0, 2).reshape(-1))
                    kt_ += bk
        return c(np.concatenate(parts))

    d = {
        "wdtT": tile_h(np.asarray(kw["W_dt"]).T),
        "bdt": tile_h(kw["b_dt"]),
        "alog": tile_h(kw["A_log"]),
        "bmat": tile_h(kw["B_mat"]),
        "cmat": tile_h(kw["C_mat"]),
        "dvec": tile_h(kw["D_vec"]),
        "wrT": tile_h(np.asarray(kw["wr"]).T),
        "rb": c(kw["rb"]),
        "wgu0": tile_wgu(kw["sg"], kw["su"]),
        "dT0": tile_d(kw["sd"]),
    }
    for e in range(ER):
        d[f"wgu{e + 1}"] = tile_wgu(kw["rg"][e], kw["ru"][e])
        d[f"dT{e + 1}"] = tile_d(kw["rd"][e])
    return d


def kernel(L0_attn_x, L0_attn_A_log, L0_attn_B_mat, L0_attn_C_mat,
           L0_attn_D_vec, L0_attn_W_dt, L0_attn_b_dt, L0_ffn_sg, L0_ffn_su,
           L0_ffn_sd, L0_ffn_rg, L0_ffn_ru, L0_ffn_rd, L0_ffn_wr, L0_ffn_rb):
    cfg = Cfg()
    n_cores = 8
    x = np.ascontiguousarray(np.asarray(L0_attn_x, dtype=np.float32))
    B, T, H = x.shape
    w = _prep_weights(cfg, {
        "W_dt": np.asarray(L0_attn_W_dt, np.float32),
        "b_dt": np.asarray(L0_attn_b_dt, np.float32),
        "A_log": np.asarray(L0_attn_A_log, np.float32),
        "B_mat": np.asarray(L0_attn_B_mat, np.float32),
        "C_mat": np.asarray(L0_attn_C_mat, np.float32),
        "D_vec": np.asarray(L0_attn_D_vec, np.float32),
        "wr": np.asarray(L0_ffn_wr, np.float32),
        "rb": np.asarray(L0_ffn_rb, np.float32),
        "sg": np.asarray(L0_ffn_sg, np.float32),
        "su": np.asarray(L0_ffn_su, np.float32),
        "sd": np.asarray(L0_ffn_sd, np.float32),
        "rg": np.asarray(L0_ffn_rg, np.float32),
        "ru": np.asarray(L0_ffn_ru, np.float32),
        "rd": np.asarray(L0_ffn_rd, np.float32),
    })
    nc = build_nc(cfg)
    in_maps = []
    for c in range(n_cores):
        xc = x[cfg.BPC * c: cfg.BPC * (c + 1)].reshape(cfg.BPC * T, H)
        m = {"x": np.ascontiguousarray(xc)}
        m.update(w)
        in_maps.append(m)
    res = run_bass_kernel_spmd(nc, in_maps, core_ids=list(range(n_cores)))
    out = np.concatenate(
        [r["out"].reshape(cfg.BPC, T, H) for r in res.results], axis=0)
    return out.astype(np.float32)


# version 31
# speedup vs baseline: 1.1338x; 1.1338x over previous
import contextlib
import numpy as np

import concourse.bass as bass
import concourse.mybir as mybir
from concourse.bass_utils import run_bass_kernel_spmd

F32 = mybir.dt.float32
BF16 = mybir.dt.bfloat16
AF = mybir.ActivationFunctionType
ALU = mybir.AluOpType
AX = mybir.AxisListType


class Cfg:
    def __init__(self, H=1024, N=16, I=2816, ER=7, T=2048, BPC=2, mini=False):
        self.H, self.N, self.I, self.ER, self.T, self.BPC = H, N, I, ER, T, BPC
        self.E = ER + 1
        self.G = H // 128
        self.KI = I // 128
        assert self.KI % 2 == 0 or True
        self.IH = (self.KI + 1) // 2
        self.IH2 = self.KI - self.IH
        self.UT = min(1024, T)
        self.NT = self.UT // 128
        self.UPB = T // self.UT
        self.UNITS = BPC * self.UPB
        self.CH = min(512, self.UT)
        self.NCH = self.UT // self.CH
        self.mini = mini



BODIES = ("tensor", "scalar", "vector", "gpsimd", "sync")


class Prog:
    def __init__(self, nc, stack):
        self.nc = nc
        self.streams = {}
        self.body_of = {}
        self.cnt = {}
        names = [("pe", "tensor"), ("act", "scalar"), ("dve", "vector"),
                 ("pool", "gpsimd")]
        names += [(f"sd{i}", "sync") for i in range(self.NSD)]
        names += [(f"gd{i}", "gpsimd") for i in range(self.NGD)]
        for s, body in names:
            self.streams[s] = stack.enter_context(nc.semaphore(f"sem_{s}"))
            self.body_of[s] = body
            self.cnt[s] = 0
        self.vc = {b: {} for b in BODIES}
        self.snap = {}
        self.last_write = {}
        self.readers = {}
        self.body_ops = {b: [] for b in BODIES}
        self._sd_rr = 0
        self._gd_rr = 0

    NSD = 5
    NGD = 6
    DMA_STREAMS = frozenset({f"sd{i}" for i in range(NSD)} |
                            {f"gd{i}" for i in range(NGD)})

    def sd_queue(self):
        self._sd_rr += 1
        return "sd%d" % (self._sd_rr % self.NSD)

    def gd_queue(self):
        self._gd_rr += 1
        return "gd%d" % (self._gd_rr % self.NGD)

    def op(self, stream, reads, writes, emit, inc=None):
        body = self.body_of[stream]
        if inc is None:
            inc = 16 if stream in self.DMA_STREAMS else 1
        if stream in self.DMA_STREAMS:
            reads = list(reads) + [("_dmaq", stream)]
            writes = list(writes) + [("_dmaq", stream)]
        needs = {}
        for k in reads:
            lw = self.last_write.get(k)
            if lw is not None:
                s2, v2 = lw
                if needs.get(s2, 0) < v2:
                    needs[s2] = v2
        for k in writes:
            lw = self.last_write.get(k)
            if lw is not None:
                s2, v2 = lw
                if needs.get(s2, 0) < v2:
                    needs[s2] = v2
            for (s2, v2) in self.readers.get(k, ()):
                if needs.get(s2, 0) < v2:
                    needs[s2] = v2
        vc = self.vc[body]
        if stream == "pe" and "pe" in needs:
            del needs["pe"]
        waits = []
        for s2 in sorted(needs):
            v2 = needs[s2]
            if vc.get(s2, 0) < v2:
                waits.append((s2, v2))
                vc[s2] = v2
                sn = self.snap.get((s2, v2))
                if sn:
                    for s3, v3 in sn.items():
                        if vc.get(s3, 0) < v3:
                            vc[s3] = v3
        self.cnt[stream] += inc
        val = self.cnt[stream]
        self.snap[(stream, val)] = dict(vc, **{stream: val})
        for k in writes:
            self.last_write[k] = (stream, val)
            self.readers[k] = []
        for k in reads:
            self.readers.setdefault(k, []).append((stream, val))
        self.body_ops[body].append((waits, emit, stream, inc))

    def emit_all(self, block, final_stores):
        sems = self.streams
        body_ops = self.body_ops

        def run(eng, body):
            for waits, emit, stream, inc in body_ops[body]:
                for s2, v2 in waits:
                    eng.wait_ge(sems[s2], v2)
                inst = emit(eng)
                if inst is not None:
                    inst.then_inc(sems[stream], inc)
            if body == "sync":
                for s2, v2 in final_stores:
                    eng.wait_ge(sems[s2], v2)

        @block.tensor
        def _(eng):
            run(eng, "tensor")

        @block.scalar
        def _(eng):
            run(eng, "scalar")

        @block.vector
        def _(eng):
            run(eng, "vector")

        @block.gpsimd
        def _(eng):
            run(eng, "gpsimd")

        @block.sync
        def _(eng):
            run(eng, "sync")



def build_nc(cfg: Cfg):
    nc = bass.Bass("TRN2", target_bir_lowering=False, debug=False)
    H, N, I, E, G, KI = cfg.H, cfg.N, cfg.I, cfg.E, cfg.G, cfg.KI
    UT, NT, CH, NCH, UNITS = cfg.UT, cfg.NT, cfg.CH, cfg.NCH, cfg.UNITS
    TOK = cfg.BPC * cfg.T

    x_hbm = nc.dram_tensor("x", [TOK, H], F32, kind="ExternalInput").ap()
    out_hbm = nc.dram_tensor("out", [TOK, H], F32, kind="ExternalOutput").ap()
    wdt_hbm = nc.dram_tensor("wdtT", [128, G * H], F32, kind="ExternalInput").ap()
    bdt_hbm = nc.dram_tensor("bdt", [128, G], F32, kind="ExternalInput").ap()
    alog_hbm = nc.dram_tensor("alog", [128, G * N], F32, kind="ExternalInput").ap()
    bmat_hbm = nc.dram_tensor("bmat", [128, G * N], F32, kind="ExternalInput").ap()
    cmat_hbm = nc.dram_tensor("cmat", [128, G * N], F32, kind="ExternalInput").ap()
    dvec_hbm = nc.dram_tensor("dvec", [128, G], F32, kind="ExternalInput").ap()
    wr_hbm = nc.dram_tensor("wrT", [128, G * (E - 1)], F32,
                            kind="ExternalInput").ap()
    rb_hbm = nc.dram_tensor("rb", [E - 1], F32, kind="ExternalInput").ap()
    wgu_hbm, d_hbm = [], []
    for e in range(E):
        wgu_hbm.append(nc.dram_tensor(f"wgu{e}", [KI * 128, G * 256], F32,
                                      kind="ExternalInput").ap())
        d_hbm.append(nc.dram_tensor(f"dT{e}", [I * H], F32,
                                    kind="ExternalInput").ap())
    wgu_bf, d_bf = [], []
    for e in range(E):
        wgu_bf.append(nc.dram_tensor(f"wgubf{e}", [KI * 128, G * 256], BF16,
                                     kind="Internal").ap())
        d_bf.append(nc.dram_tensor(f"dbf{e}", [I * H], BF16,
                                   kind="Internal").ap())

    stack = contextlib.ExitStack()
    sb = lambda name, shape, dt: stack.enter_context(nc.sbuf_tensor(name, shape, dt))
    psb = lambda name, shape, dt: stack.enter_context(nc.psum_tensor(name, shape, dt))

    xnT = sb("xnT", [128, G, UT], BF16)
    act = sb("act", [128, cfg.IH, UT], BF16)
    acc = sb("acc", [128, NT, H], F32)
    ytok = sb("ytok", [128, NT, H], BF16)
    dtbf = sb("dtbf", [128, G, UT], BF16)
    wdtT = sb("wdtT_sb", [128, G, H], BF16)
    du = [sb(f"du{i}", [128, UT], F32) for i in range(2)]
    NAB = 3
    abuf = [sb(f"abuf{i}", [128, CH], F32) for i in range(NAB)]
    bbuf = [sb(f"bbuf{i}", [128, CH], F32) for i in range(2)]
    sbuf_s = [sb(f"sbuf{i}", [128, CH], F32) for i in range(2)]
    ysl = [sb(f"ysl{i}", [128, UT], F32) for i in range(2)]
    tsb = [sb(f"tsb{i}", [128, CH], F32) for i in range(2)]
    WGB = min(128, I)
    NWB = 3
    wgu = [sb(f"wgub{i}", [128, G, 256], BF16) for i in range(NWB)]
    WDB = min(4, KI)
    NWD = 4
    wd = [sb(f"wd{i}", [128, WDB, 512 if H >= 1024 else H], BF16)
          for i in range(NWD)]
    xns = [sb(f"xns{i}", [128, H], BF16) for i in range(2)]
    xres = [sb(f"xres{i}", [128, min(512, H)], F32) for i in range(2)]
    xb16 = sb("xb16", [128, NT, H], BF16)
    stats = sb("stats", [128, 64], F32)
    rt = sb("rt", [128, 4 * (E - 1)], F32)
    rsc = sb("rsc", [128, 16], F32)
    wtok = sb("wtok", [128, NT, E - 1], F32)
    a_neg = sb("a_neg", [128, G, N], F32)
    cb = sb("cb", [128, G, N], F32)
    dv = sb("dv", [128, G], F32)
    bdt_sb = sb("bdt_sb", [128, G], F32)
    carry = sb("carry", [128, G, N], F32)
    wrT_sb = sb("wrT_sb", [128, G, E - 1], BF16)
    rb_row = sb("rb_row", [1, E - 1], F32)
    ones_row = sb("ones_row", [1, 128], F32)
    ident_bf = sb("ident_bf", [128, 128], BF16)
    ident_f = sb("ident_f", [128, 128], F32)
    ones_col = sb("ones_col", [128, 1], F32)

    ps_gate = [psb(f"ps_gate{i}", [128, CH], F32) for i in range(2)]
    ps_up = [psb(f"ps_up{i}", [128, CH], F32) for i in range(2)]
    ps_dn = [psb(f"ps_dn{i}", [128, min(512, H)], F32) for i in range(2)]
    GROT = [ps_gate[0], ps_gate[1]]
    GROTK = [("ps_gate", 0), ("ps_gate", 1)]
    UROT = [ps_up[0], ps_up[1]]
    UROTK = [("ps_up", 0), ("ps_up", 1)]
    DROT = [ps_dn[0], ps_dn[1], ps_up[0], ps_up[1], ps_gate[0], ps_gate[1]]
    DROTK = [("ps_dn", 0), ("ps_dn", 1), ("ps_up", 0), ("ps_up", 1),
             ("ps_gate", 0), ("ps_gate", 1)]
    ps_m32 = psb("ps_m32", [128, 512], F32)
    ps_mbf = psb("ps_mbf", [128, 1024], BF16)

    p = Prog(nc, stack)
    K_CONST = ("const",)

    def dma(stream_pick, dst_ap, src_ap, reads, writes, slow=False):
        q = stream_pick()
        if slow:
            p.op(q, reads, writes,
                 (lambda e, d=dst_ap, s=src_ap: e.dma_start(
                     out=d, in_=s, allow_slow_non_contiguous=True)))
        else:
            p.op(q, reads, writes,
                 (lambda e, d=dst_ap, s=src_ap: e.dma_start(out=d, in_=s)))

    p.op("pool", [], [K_CONST + ("idb",)], lambda e: e.memset(ident_bf[:], 0.0))
    p.op("pool", [K_CONST + ("idb",)], [K_CONST + ("idb",)],
         lambda e: e.affine_select(out=ident_bf[:], in_=ident_bf[:],
                                   compare_op=ALU.not_equal, fill=1.0, base=0,
                                   pattern=[[-1, 128]], channel_multiplier=1))
    p.op("pool", [], [K_CONST + ("idf",)], lambda e: e.memset(ident_f[:], 0.0))
    p.op("pool", [K_CONST + ("idf",)], [K_CONST + ("idf",)],
         lambda e: e.affine_select(out=ident_f[:], in_=ident_f[:],
                                   compare_op=ALU.not_equal, fill=1.0, base=0,
                                   pattern=[[-1, 128]], channel_multiplier=1))
    p.op("pool", [], [K_CONST + ("ones",)], lambda e: e.memset(ones_col[:], 1.0))

    dma(p.gd_queue, wdtT[:],
        wdt_hbm.rearrange("p (kk g) -> p kk g", kk=G),
        [], [("wdtT",)])
    dma(p.gd_queue, a_neg[:], alog_hbm.rearrange("p (g n) -> p g n", g=G),
        [], [K_CONST + ("alog",)])
    dma(p.gd_queue, cb[:], bmat_hbm.rearrange("p (g n) -> p g n", g=G),
        [], [K_CONST + ("cbB",)])
    dma(p.gd_queue, ysl[0][:, 0:G * N].rearrange("p (g n) -> p g n", g=G),
        cmat_hbm.rearrange("p (g n) -> p g n", g=G),
        [], [("ysl", 0)])
    dma(p.gd_queue, dv[:], dvec_hbm, [], [K_CONST + ("dv",)])
    dma(p.gd_queue, bdt_sb[:], bdt_hbm, [], [K_CONST + ("bdt",)])
    dma(p.gd_queue, wrT_sb[:], wr_hbm.rearrange("p (kk e) -> p kk e", kk=G),
        [], [K_CONST + ("wrT",)])
    dma(p.gd_queue, rb_row[:], rb_hbm[None, :], [], [K_CONST + ("rb",)])
    p.op("pool", [], [K_CONST + ("ones_row",)],
         lambda e: e.memset(ones_row[:], 1.0))
    p.op("act", [K_CONST + ("alog",)], [K_CONST + ("alog",)],
         lambda e: e.activation(out=a_neg[:], in_=a_neg[:], func=AF.Exp))
    p.op("dve", [K_CONST + ("alog",)], [K_CONST + ("A",)],
         lambda e: e.tensor_scalar(out=a_neg[:], in0=a_neg[:], scalar1=-1.0,
                                   scalar2=None, op0=ALU.mult))
    p.op("dve", [K_CONST + ("cbB",), ("ysl", 0)], [K_CONST + ("cb",)],
         lambda e: e.tensor_tensor(
             out=cb[:], in0=cb[:],
             in1=ysl[0][:, 0:G * N].rearrange("p (g n) -> p g n", g=G),
             op=ALU.mult))

    def ln_steps_a(u, i, xrow0):
        bi = i % 2
        kx = ("xb16", i)
        kst = ("st", i)
        so = 8 * i
        st = []
        st.append(lambda: dma(p.gd_queue, xb16[:, i, :],
                              x_hbm[xrow0:xrow0 + 128, :], [], [kx]))
        st.append(lambda: p.op("act", [kx], [("xns", bi), kst],
             lambda e: e.activation(
                 out=xns[bi][:], in_=xb16[:, i, :], func=AF.Identity,
                 accum_out=stats[:, so + 0:so + 1])))
        st.append(lambda: p.op("act", [kst], [kst],
             lambda e: e.mul(stats[:, so + 1:so + 2],
                             stats[:, so + 0:so + 1], -1.0 / H)))
        st.append(lambda: p.op("act", [kx, kst], [("xns", bi), kst],
             lambda e: e.activation(
                 out=xns[bi][:], in_=xb16[:, i, :], func=AF.Square,
                 bias=stats[:, so + 1:so + 2],
                 accum_out=stats[:, so + 2:so + 3])))
        st.append(lambda: p.op("dve", [kst], [kst],
             lambda e: e.tensor_scalar(
                 out=stats[:, so + 3:so + 4], in0=stats[:, so + 2:so + 3],
                 scalar1=1.0 / H, scalar2=1e-5, op0=ALU.mult, op1=ALU.add)))
        st.append(lambda: p.op("act", [kst], [kst],
             lambda e: e.activation(
                 out=stats[:, so + 4:so + 5], in_=stats[:, so + 3:so + 4],
                 func=AF.Ln)))
        st.append(lambda: p.op("act", [kst], [kst],
             lambda e: e.activation(
                 out=stats[:, so + 5:so + 6], in_=stats[:, so + 4:so + 5],
                 func=AF.Exp, scale=-0.5)))
        st.append(lambda: p.op("dve", [kst], [kst],
             lambda e: e.tensor_tensor(
                 out=stats[:, so + 6:so + 7], in0=stats[:, so + 1:so + 2],
                 in1=stats[:, so + 5:so + 6], op=ALU.mult)))
        return st

    def ln_steps_b(u, i, xrow0):
        bi = i % 2
        kst = ("st", i)
        so = 8 * i
        st = []
        st.append(lambda: p.op("act", [("xb16", i), kst], [("xns", bi)],
             lambda e: e.activation(
                 out=xns[bi][:], in_=xb16[:, i, :], func=AF.Identity,
                 scale=stats[:, so + 5:so + 6], bias=stats[:, so + 6:so + 7])))

        def tr(e):
            last = None
            for j in range(G):
                last = e.transpose(ps_mbf[:, 128 * j:128 * (j + 1)],
                                   xns[bi][:, 128 * j:128 * (j + 1)],
                                   ident_bf[:])
            return last
        st.append(lambda: p.op("pe", [("xns", bi), K_CONST + ("idb",)],
                               [("ps_mbf",)], tr))
        st.append(lambda: p.op("dve", [("ps_mbf",)], [("xnT", i)],
             lambda e: e.tensor_copy(
                 out=xnT[:, :, 128 * i:128 * (i + 1)],
                 in_=ps_mbf[:, 0:128 * G].rearrange("p (g c) -> p g c", g=G))))
        return st

    def router_steps(u, i):
        bi = i % 2
        ER = E - 1
        ko = ("rt", bi)
        krs = ("rsc", bi)
        l_sl = rt[:, bi * 2 * ER: bi * 2 * ER + ER]
        e_sl = rt[:, bi * 2 * ER + ER: bi * 2 * ER + 2 * ER]
        r0 = bi * 8
        st = []

        def mm(e):
            for kk in range(G):
                e.matmul(ps_m32[:, 0:ER],
                         xnT[:, kk, 128 * i:128 * (i + 1)],
                         wrT_sb[:, kk, :],
                         start=(kk == 0), stop=False)
            return e.matmul(ps_m32[:, 0:ER], ones_row[:], rb_row[:],
                            start=False, stop=True)
        st.append(lambda: p.op("pe", [("xnT", i), K_CONST + ("wrT",),
                                      K_CONST + ("rb",),
                                      K_CONST + ("ones_row",)],
                               [("ps_m32",)], mm))
        st.append(lambda: p.op("dve", [("ps_m32",)], [ko],
             lambda e: e.tensor_copy(out=l_sl, in_=ps_m32[:, 0:E - 1])))
        st.append(lambda: p.op("dve", [ko], [krs],
             lambda e: e.tensor_reduce(
                 out=rsc[:, r0:r0 + 1], in_=l_sl, axis=AX.X, op=ALU.max)))
        st.append(lambda: p.op("dve", [ko, krs], [ko],
             lambda e: e.tensor_scalar(
                 out=e_sl, in0=l_sl, scalar1=rsc[:, r0:r0 + 1], scalar2=None,
                 op0=ALU.subtract)))
        st.append(lambda: p.op("act", [ko], [ko],
             lambda e: e.activation(out=e_sl, in_=e_sl, func=AF.Exp)))
        st.append(lambda: p.op("dve", [ko], [ko],
             lambda e: e.tensor_scalar(
                 out=l_sl, in0=e_sl, scalar1=1.0, scalar2=None,
                 op0=ALU.is_lt)))
        st.append(lambda: p.op("dve", [ko], [ko],
             lambda e: e.tensor_tensor(
                 out=l_sl, in0=l_sl, in1=e_sl, op=ALU.mult)))
        st.append(lambda: p.op("dve", [ko], [krs],
             lambda e: e.tensor_reduce(
                 out=rsc[:, r0 + 1:r0 + 2], in_=l_sl, axis=AX.X, op=ALU.max)))
        st.append(lambda: p.op("dve", [krs], [krs],
             lambda e: e.tensor_scalar(
                 out=rsc[:, r0 + 2:r0 + 3], in0=rsc[:, r0 + 1:r0 + 2],
                 scalar1=1.0, scalar2=None, op0=ALU.add)))
        st.append(lambda: p.op("dve", [krs], [krs],
             lambda e: e.reciprocal(out=rsc[:, r0 + 3:r0 + 4],
                                    in_=rsc[:, r0 + 2:r0 + 3])))
        st.append(lambda: p.op("dve", [ko, krs], [("wtok", i)],
             lambda e: e.tensor_scalar(
                 out=wtok[:, i, :], in0=e_sl, scalar1=rsc[:, r0 + 1:r0 + 2],
                 scalar2=None, op0=ALU.is_ge)))
        st.append(lambda: p.op("dve", [ko, ("wtok", i)], [("wtok", i)],
             lambda e: e.tensor_tensor(
                 out=wtok[:, i, :], in0=wtok[:, i, :], in1=e_sl,
                 op=ALU.mult)))
        st.append(lambda: p.op("dve", [krs, ("wtok", i)], [("wtok", i)],
             lambda e: e.tensor_scalar(
                 out=wtok[:, i, :], in0=wtok[:, i, :],
                 scalar1=rsc[:, r0 + 3:r0 + 4], scalar2=None, op0=ALU.mult)))
        return st

    def dt_slab(u, g):
        for c in range(NCH):
            xnT_keys = [("xnT", i) for i in range(NT)
                        if CH * c <= 128 * i < CH * (c + 1)]
            pk = ("ps_dn", c % 2)

            def mm(e, g=g, c=c, pk=pk):
                last = None
                for kk in range(G):
                    last = e.matmul(ps_dn[c % 2][:, 0:CH],
                                    wdtT[:, kk, 128 * g:128 * (g + 1)],
                                    xnT[:, kk, CH * c:CH * (c + 1)],
                                    start=(kk == 0), stop=(kk == G - 1))
                return last
            p.op("pe", xnT_keys + [("wdtT",)], [pk], mm)
            ka = ("a", (2 * g + c) % NAB)
            p.op("act", [pk, K_CONST + ("bdt",)], [ka],
                 lambda e, g=g, c=c, ka=ka: e.activation(
                     out=abuf[ka[1]][:], in_=ps_dn[c % 2][:, 0:CH], func=AF.Exp,
                     bias=bdt_sb[:, g:g + 1]))
            p.op("act", [ka, K_CONST + ("ones",)], [("dtbf", g, c)],
                 lambda e, g=g, c=c, ka=ka: e.activation(
                     out=dtbf[:, g, CH * c:CH * (c + 1)], in_=abuf[ka[1]][:],
                     func=AF.Ln, bias=ones_col[:]))

    def scan_slab_thunks(u, g, first_unit_of_batch):
        gb = g % 2
        kdu = ("du", gb)
        kysl = ("ysl", gb)
        thunks = []

        def t_init():
            p.op("dve", [("dtbf", g, 0), ("dtbf", g, min(1, NCH - 1))] +
                 [("xnT", i) for i in range(NT)], [kdu],
                 lambda e, g=g, gb=gb: e.tensor_tensor(
                     out=du[gb][:], in0=dtbf[:, g, :], in1=xnT[:, g, :],
                     op=ALU.mult))
            for c in range(NCH):
                p.op("act", [("xnT", i) for i in range(NT)] +
                     [K_CONST + ("dv",)], [kysl],
                     lambda e, g=g, c=c, gb=gb: e.activation(
                         out=ysl[gb][:, CH * c:CH * (c + 1)],
                         in_=xnT[:, g, CH * c:CH * (c + 1)], func=AF.Identity,
                         scale=dv[:, g:g + 1]))
            if first_unit_of_batch:
                p.op("dve", [], [("carry", g)],
                     lambda e, g=g: e.memset(carry[:, g, :], 0.0))
        thunks.append(t_init)

        def t_iter(n, c):
            ia = (n * NCH + c) % NAB
            ka = ("a", ia)
            p.op("act", [("dtbf", g, c), K_CONST + ("A",)], [ka],
                 lambda e, g=g, c=c, n=n, ia=ia: e.activation(
                     out=abuf[ia][:], in_=dtbf[:, g, CH * c:CH * (c + 1)],
                     func=AF.Exp, scale=a_neg[:, g, n:n + 1]))
            ib2 = (n * NCH + c) % 2
            kb = ("b", ib2)
            p.op("act", [kdu, K_CONST + ("cb",)], [kb],
                 lambda e, ib2=ib2, g=g, n=n, c=c, gb=gb: e.activation(
                     out=bbuf[ib2][:], in_=du[gb][:, CH * c:CH * (c + 1)],
                     func=AF.Copy, scale=cb[:, g, n:n + 1]))
            isb = (n * NCH + c) % 2
            ks = ("s", isb)
            if c == 0:
                init_ap = carry[:, g, n:n + 1]
                init_key = ("carry", g)
            else:
                prev = (n * NCH + c - 1) % 2
                init_ap = sbuf_s[prev][:, CH - 1:CH]
                init_key = ("s", prev)
            p.op("dve", [ka, kb, init_key], [ks],
                 lambda e, ia=ia, ib2=ib2, isb=isb, init_ap=init_ap:
                 e.tensor_tensor_scan(
                     out=sbuf_s[isb][:], data0=abuf[ia][:],
                     data1=bbuf[ib2][:],
                     initial=init_ap, op0=ALU.mult, op1=ALU.add))
            p.op("dve", [ks, kysl], [kysl],
                 lambda e, isb=isb, c=c, gb=gb: e.tensor_tensor(
                     out=ysl[gb][:, CH * c:CH * (c + 1)],
                     in0=ysl[gb][:, CH * c:CH * (c + 1)],
                     in1=sbuf_s[isb][:], op=ALU.add))
            if c == NCH - 1:
                p.op("dve", [ks], [("carry", g)],
                     lambda e, isb=isb, g=g, n=n: e.tensor_copy(
                         out=carry[:, g, n:n + 1],
                         in_=sbuf_s[isb][:, CH - 1:CH]))
        for n in range(N):
            for c in range(NCH):
                thunks.append(lambda n=n, c=c: t_iter(n, c))

        def t_tr(j0, j1):
            def tr(e, j0=j0, j1=j1, gb=gb):
                last = None
                for j in range(j0, j1):
                    last = e.transpose(
                        ps_m32[:, 128 * (j - j0):128 * (j - j0 + 1)],
                        ysl[gb][:, 128 * j:128 * (j + 1)], ident_f[:])
                return last
            p.op("pe", [kysl, K_CONST + ("idf",)], [("ps_m32",)], tr)
            p.op("dve", [("ps_m32",)],
                 [("ytok", j, g) for j in range(j0, j1)],
                 lambda e, j0=j0, j1=j1, g=g: e.tensor_copy(
                     out=ytok[:, j0:j1, 128 * g:128 * (g + 1)],
                     in_=ps_m32[:, 0:128 * (j1 - j0)].rearrange(
                         "p (a b) -> p a b", a=j1 - j0)))
        nblk = UT // 128
        tr_thunks = []
        for half in range((nblk + 3) // 4):
            j0, j1 = 4 * half, min(4 * half + 4, nblk)
            tr_thunks.append(lambda j0=j0, j1=j1: t_tr(j0, j1))
        return thunks, tr_thunks

    def gateup_blocks(it0, it1):
        blocks = []
        col = it0 * 128
        while col < it1 * 128:
            bw = min(WGB, it1 * 128 - col)
            blocks.append((col, bw))
            col += bw
        return blocks

    class WeightWheel:

        def __init__(self, items, ahead=3):
            self.items = items
            self.ahead = ahead
            self.emitted = 0
            self.consumed = 0

        def _pump(self, target):
            while self.emitted < min(len(self.items), target):
                u_i, e_i, col, bw = self.items[self.emitted]
                seq = self.emitted
                b = col // 128
                if u_i == 0:
                    dma(p.gd_queue, wgu[seq % NWB][:],
                        wgu_hbm[e_i][128 * b:128 * (b + 1), :]
                        .rearrange("p (kk c) -> p kk c", kk=G),
                        [], [("wgu", seq % NWB)])
                    dma(p.sd_queue,
                        wgu_bf[e_i][128 * b:128 * (b + 1), :]
                        .rearrange("p (kk c) -> p kk c", kk=G),
                        wgu[seq % NWB][:],
                        [("wgu", seq % NWB)], [("gbf", e_i, b)])
                else:
                    dma(p.sd_queue, wgu[seq % NWB][:],
                        wgu_bf[e_i][128 * b:128 * (b + 1), :]
                        .rearrange("p (kk c) -> p kk c", kk=G),
                        [("gbf", e_i, b)], [("wgu", seq % NWB)])
                self.emitted += 1

        def ensure_ahead(self):
            s = self.consumed
            self.consumed = s + 1
            self._pump(s + self.ahead)
            return s

        def prefetch(self):
            self._pump(self.consumed + self.ahead)

    def _wd_offsets():
        HH0 = min(512, H)
        off = {}
        pos = 0
        for hidx, (it0, it1) in enumerate([(0, cfg.IH), (cfg.IH, KI)]):
            for hh in range(H // HH0):
                kt_ = it0
                bidx = 0
                while kt_ < it1:
                    bk = min(WDB, it1 - kt_)
                    off[(hidx, hh, bidx)] = (pos, kt_, bk)
                    pos += 128 * bk * HH0
                    kt_ += bk
                    bidx += 1
        return off
    WD_OFF = _wd_offsets()

    def ffn_expert(u, e_idx, drain, wheel):
        dhbm = d_hbm[e_idx]
        xnT_chunk_keys = [[("xnT", i) for i in range(NT)
                           if CH * c <= 128 * i < CH * (c + 1)]
                          for c in range(NCH)]
        halves = [(0, cfg.IH), (cfg.IH, KI)]
        for hidx, (it0, it1) in enumerate(halves):
            if it1 <= it0:
                continue
            ntile_half = it1 - it0
            HH0 = min(512, H)
            dblocks_by_hh = {}
            for hh in range(H // HH0):
                dblocks = []
                kt_ = it0
                while kt_ < it1:
                    bk = min(WDB, it1 - kt_)
                    dblocks.append((kt_, bk))
                    kt_ += bk
                dblocks_by_hh[hh] = dblocks

            nblk_hh = len(dblocks_by_hh[0])

            def emit_wd_dmas(hh, only=None):
                for bidx, (kt0, bk) in enumerate(dblocks_by_hh[hh]):
                    wdi = (nblk_hh * hh + bidx) % NWD
                    if only is not None and wdi not in only:
                        continue
                    pos, _kt0, _bk = WD_OFF[(hidx, hh, bidx)]
                    kbf = ("dbf", e_idx, hidx, hh, bidx)
                    if u == 0:
                        dma(p.gd_queue, wd[wdi][:, 0:bk, 0:HH0],
                            dhbm[pos:pos + 128 * bk * HH0]
                            .rearrange("(p kk h) -> p kk h", p=128, kk=bk),
                            [], [("wd", wdi)])
                        dma(p.sd_queue,
                            d_bf[e_idx][pos:pos + 128 * bk * HH0]
                            .rearrange("(p kk h) -> p kk h", p=128, kk=bk),
                            wd[wdi][:, 0:bk, 0:HH0],
                            [("wd", wdi)], [kbf])
                    else:
                        dma(p.sd_queue, wd[wdi][:, 0:bk, 0:HH0],
                            d_bf[e_idx][pos:pos + 128 * bk * HH0]
                            .rearrange("(p kk h) -> p kk h", p=128, kk=bk),
                            [kbf], [("wd", wdi)])
            emit_wd_dmas(0)
            blocks = gateup_blocks(it0, it1)
            for bidx, (col, bw) in enumerate(blocks):
                seq = wheel.ensure_ahead()
                kwg = ("wgu", seq % NWB)
                kwu = kwg
                for sub in range(bw // 128):
                    it = (col + 128 * sub) // 128
                    lt = it - it0
                    for c in range(NCH):
                        ri = (it * NCH + c) % len(GROT)
                        pg = GROTK[ri]
                        pu = UROTK[ri]

                        def mmg(e, sub=sub, c=c, ri=ri, seq=seq):
                            last = None
                            for kk in range(G):
                                last = e.matmul(
                                    GROT[ri][:],
                                    wgu[seq % NWB][:, kk, 0:128],
                                    xnT[:, kk, CH * c:CH * (c + 1)],
                                    start=(kk == 0), stop=(kk == G - 1))
                            return last
                        p.op("pe", [kwg] + xnT_chunk_keys[c], [pg], mmg)

                        def mmu(e, sub=sub, c=c, ri=ri, seq=seq):
                            last = None
                            for kk in range(G):
                                last = e.matmul(
                                    UROT[ri][:],
                                    wgu[seq % NWB][:, kk, 128:256],
                                    xnT[:, kk, CH * c:CH * (c + 1)],
                                    start=(kk == 0), stop=(kk == G - 1))
                            return last
                        p.op("pe", [kwu] + xnT_chunk_keys[c], [pu], mmu)
                        kt = ("tsb", (it * NCH + c) % 2)
                        p.op("act", [pg], [kt],
                             lambda e, it=it, c=c, ri=ri: e.activation(
                                 out=tsb[(it * NCH + c) % 2][:],
                                 in_=GROT[ri][:],
                                 func=AF.Tanh, scale=0.5))
                        p.op("dve", [kt, pg], [kt],
                             lambda e, it=it, c=c, ri=ri: e.scalar_tensor_tensor(
                                 out=tsb[(it * NCH + c) % 2][:],
                                 in0=tsb[(it * NCH + c) % 2][:], scalar=1.0,
                                 in1=GROT[ri][:],
                                 op0=ALU.add, op1=ALU.mult))
                        p.op("dve", [kt, pu], [("act", lt, c)],
                             lambda e, it=it, c=c, lt=lt, ri=ri:
                             e.scalar_tensor_tensor(
                                 out=act[:, lt, CH * c:CH * (c + 1)],
                                 in0=tsb[(it * NCH + c) % 2][:], scalar=0.5,
                                 in1=UROT[ri][:],
                                 op0=ALU.mult, op1=ALU.mult))
                        drain(1)
            HH = min(512, H)
            nhh = H // HH
            def finish_tg(tg, hh):
                di = tg % min(4, NT, len(DROT))
                pk = DROTK[di]
                kacc = ("acc", tg, hh)
                if e_idx == 0 and hidx == 0:
                    p.op("dve", [pk], [kacc],
                         lambda e, tg=tg, hh=hh, di=di: e.tensor_copy(
                             out=acc[:, tg, HH * hh:HH * (hh + 1)],
                             in_=DROT[di][:, 0:HH]))
                elif e_idx == 0:
                    p.op("dve", [pk, kacc], [kacc],
                         lambda e, tg=tg, hh=hh, di=di: e.tensor_tensor(
                             out=acc[:, tg, HH * hh:HH * (hh + 1)],
                             in0=acc[:, tg, HH * hh:HH * (hh + 1)],
                             in1=DROT[di][:, 0:HH], op=ALU.add))
                else:
                    p.op("dve", [pk, kacc, ("wtok", tg)], [kacc],
                         lambda e, tg=tg, hh=hh, e_idx=e_idx, di=di:
                         e.scalar_tensor_tensor(
                             out=acc[:, tg, HH * hh:HH * (hh + 1)],
                             in0=DROT[di][:, 0:HH],
                             scalar=wtok[:, tg, e_idx - 1:e_idx],
                             in1=acc[:, tg, HH * hh:HH * (hh + 1)],
                             op0=ALU.mult, op1=ALU.add))

            for hh in range(nhh):
                wheel.prefetch()
                cur_bufs = {(nblk_hh * hh + b) % NWD
                            for b in range(len(dblocks_by_hh[hh]))}
                nxt_bufs = {(nblk_hh * (hh + 1) + b) % NWD
                            for b in range(len(dblocks_by_hh[hh]))}
                if hh + 1 < nhh:
                    emit_wd_dmas(hh + 1, only=nxt_bufs - cur_bufs)
                dblocks = dblocks_by_hh[hh]
                WAVE = min(4, NT, len(DROT))

                def emit_dblock(tg, bidx, di):
                    kt0, bk = dblocks[bidx]
                    pk = DROTK[di]

                    def mmd(e, tg=tg, hh=hh, it0=it0, it1=it1,
                            kt0=kt0, bk=bk, bidx=bidx, di=di):
                        last = None
                        for kk in range(bk):
                            kt_g = kt0 + kk
                            last = e.matmul(
                                DROT[di][:, 0:HH],
                                act[:, kt_g - it0, 128 * tg:128 * (tg + 1)],
                                wd[(nblk_hh * hh + bidx) % NWD][:, kk, 0:HH],
                                start=(kt_g == it0), stop=(kt_g == it1 - 1))
                        return last
                    blk_act_keys = [("act", kt0 - it0 + kk, c)
                                    for kk in range(bk) for c in range(NCH)]
                    p.op("pe", blk_act_keys +
                         [("wd", (nblk_hh * hh + bidx) % NWD), pk], [pk], mmd)

                for w0 in range(0, NT, WAVE):
                    wave = range(w0, min(w0 + WAVE, NT))
                    for tg in wave:
                        for bidx in range(len(dblocks) - 1):
                            emit_dblock(tg, bidx, tg % WAVE)
                    for tg in wave:
                        emit_dblock(tg, len(dblocks) - 1, tg % WAVE)
                        finish_tg(tg, hh)
                        drain(1)
                if hh + 1 < nhh:
                    emit_wd_dmas(hh + 1, only=nxt_bufs & cur_bufs)

    def d_tile_thunks(u, i, xrow0):
        return [lambda: d_tile_calc(u, i, 0, xrow0),
                (lambda: d_tile_calc(u, i, 1, xrow0)) if H > 512 else None]

    def d_tile_calc(u, i, hh, xrow0):
        HH = min(512, H)
        ia = hh % 2
        dma(p.sd_queue, xres[ia][:, 0:HH],
            x_hbm[xrow0:xrow0 + 128, HH * hh:HH * (hh + 1)],
            [], [("xres", ia)])
        ka = ("a", ia)
        p.op("act", [("acc", i, hh)], [ka],
             lambda e, i=i, hh=hh, ia=ia: e.activation(
                 out=abuf[ia][:, 0:HH], in_=acc[:, i, HH * hh:HH * (hh + 1)],
                 func=AF.Tanh, scale=0.5))
        ksv = ("s", ia % 2)
        p.op("dve", [ka, ("acc", i, hh)], [ksv],
             lambda e, i=i, hh=hh, ia=ia: e.scalar_tensor_tensor(
                 out=sbuf_s[ia % 2][:, 0:HH], in0=abuf[ia][:, 0:HH],
                 scalar=1.0, in1=acc[:, i, HH * hh:HH * (hh + 1)],
                 op0=ALU.add, op1=ALU.mult))
        ytk = [("ytok", i, g2) for g2 in range(G)
               if HH * hh <= 128 * g2 < HH * (hh + 1)]
        p.op("dve", [ksv] + ytk, [ksv],
             lambda e, i=i, hh=hh, ia=ia: e.scalar_tensor_tensor(
                 out=sbuf_s[ia % 2][:, 0:HH], in0=sbuf_s[ia % 2][:, 0:HH],
                 scalar=0.5, in1=ytok[:, i, HH * hh:HH * (hh + 1)],
                 op0=ALU.mult, op1=ALU.mult))
        kxr = ("xres", ia % 2)
        p.op("dve", [ksv, kxr], [ksv],
             lambda e, i=i, hh=hh, ia=ia: e.tensor_tensor(
                 out=sbuf_s[ia % 2][:, 0:HH], in0=sbuf_s[ia % 2][:, 0:HH],
                 in1=xres[ia % 2][:, 0:HH], op=ALU.add))
        p.op(p.sd_queue(), [ksv], [("outdma", u, i, hh)],
             lambda e, i=i, hh=hh, ia=ia, xrow0=xrow0: e.dma_start(
                 out=out_hbm[xrow0:xrow0 + 128, HH * hh:HH * (hh + 1)],
                 in_=sbuf_s[ia % 2][:, 0:HH]))

    from collections import deque
    pending = deque()

    witems = []
    for _u in range(UNITS):
        for e_i in range(E):
            for (it0, it1) in [(0, cfg.IH), (cfg.IH, KI)]:
                if it1 > it0:
                    for (col, bw) in gateup_blocks(it0, it1):
                        witems.append((_u, e_i, col, bw))
    wheel = WeightWheel(witems)

    def drain(k):
        for _ in range(k):
            if not pending:
                return
            t = pending.popleft()
            if t is not None:
                t()

    for u in range(UNITS):
        base = u * UT
        first_of_batch = (u % cfg.UPB) == 0
        import itertools
        if u == 0:
            for i in range(NT):
                for t in ln_steps_a(0, i, 128 * i):
                    t()
        for i0 in range(0, NT, 2):
            sa = ln_steps_b(u, i0, base + 128 * i0) + router_steps(u, i0)
            sb = (ln_steps_b(u, i0 + 1, base + 128 * (i0 + 1)) +
                  router_steps(u, i0 + 1)) if i0 + 1 < NT else []
            OFF = 2
            for k in range(len(sa) + OFF):
                if k < len(sa):
                    sa[k]()
                if 0 <= k - OFF < len(sb):
                    sb[k - OFF]()
            drain(2)
        if u > 0:
            for i in range(NT):
                pending.extend(d_tile_thunks(u - 1, i, (u - 1) * UT + 128 * i))
        for g in range(G):
            dt_slab(u, g)
            drain(2)
        tr_pending = None

        def pend_slab(g):
            nonlocal tr_pending
            it_th, tr_th = scan_slab_thunks(u, g, first_of_batch)
            pending.extend(it_th)
            if tr_pending:
                pending.extend(tr_pending)
            tr_pending = tr_th
        for e_idx in range(E):
            if e_idx == min(5, E - 1) and u + 1 < UNITS:
                for i in range(NT):
                    pending.extend(ln_steps_a(u + 1, i,
                                              (u + 1) * UT + 128 * i))
            if e_idx < G:
                pend_slab(e_idx)
            ffn_expert(u, e_idx, drain, wheel)
        for g in range(E, G):
            pend_slab(g)
        if tr_pending:
            pending.extend(tr_pending)
        drain(len(pending))
    u = UNITS - 1
    for i in range(NT):
        pending.extend(d_tile_thunks(u, i, u * UT + 128 * i))
    drain(len(pending))

    finals = []
    for s2 in sorted(Prog.DMA_STREAMS):
        if p.cnt[s2]:
            finals.append((s2, p.cnt[s2]))
    with nc.Block() as block:
        p.emit_all(block, finals)
    stack.close()
    return nc



def _prep_weights(cfg, kw):
    c = np.ascontiguousarray
    H, I, G, N, KI = cfg.H, cfg.I, cfg.G, cfg.N, cfg.KI
    ER = cfg.E - 1

    def tile_h(a):
        a = np.asarray(a)
        rest = a.shape[1:]
        out = a.reshape(G, 128, *rest).transpose(1, 0, *range(2, a.ndim + 1))
        return c(out.reshape(128, -1))

    def tile_wgu(g, u):
        gt = g.T.reshape(G, 128, KI, 128)
        ut = u.T.reshape(G, 128, KI, 128)
        st = np.concatenate([gt, ut], axis=-1)
        return c(st.transpose(2, 1, 0, 3).reshape(KI * 128, G * 256))

    def tile_d(dmat):
        dT = dmat.T
        HH0 = min(512, H)
        IH = cfg.IH
        WDB = min(4, KI)
        parts = []
        for (it0, it1) in [(0, IH), (IH, KI)]:
            for hh in range(H // HH0):
                kt_ = it0
                while kt_ < it1:
                    bk = min(WDB, it1 - kt_)
                    blk = dT[128 * kt_:128 * (kt_ + bk),
                             HH0 * hh:HH0 * (hh + 1)]
                    parts.append(blk.reshape(bk, 128, HH0)
                                 .transpose(1, 0, 2).reshape(-1))
                    kt_ += bk
        return c(np.concatenate(parts))

    d = {
        "wdtT": tile_h(np.asarray(kw["W_dt"]).T),
        "bdt": tile_h(kw["b_dt"]),
        "alog": tile_h(kw["A_log"]),
        "bmat": tile_h(kw["B_mat"]),
        "cmat": tile_h(kw["C_mat"]),
        "dvec": tile_h(kw["D_vec"]),
        "wrT": tile_h(np.asarray(kw["wr"]).T),
        "rb": c(kw["rb"]),
        "wgu0": tile_wgu(kw["sg"], kw["su"]),
        "dT0": tile_d(kw["sd"]),
    }
    for e in range(ER):
        d[f"wgu{e + 1}"] = tile_wgu(kw["rg"][e], kw["ru"][e])
        d[f"dT{e + 1}"] = tile_d(kw["rd"][e])
    return d


def kernel(L0_attn_x, L0_attn_A_log, L0_attn_B_mat, L0_attn_C_mat,
           L0_attn_D_vec, L0_attn_W_dt, L0_attn_b_dt, L0_ffn_sg, L0_ffn_su,
           L0_ffn_sd, L0_ffn_rg, L0_ffn_ru, L0_ffn_rd, L0_ffn_wr, L0_ffn_rb):
    cfg = Cfg()
    n_cores = 8
    x = np.ascontiguousarray(np.asarray(L0_attn_x, dtype=np.float32))
    B, T, H = x.shape
    w = _prep_weights(cfg, {
        "W_dt": np.asarray(L0_attn_W_dt, np.float32),
        "b_dt": np.asarray(L0_attn_b_dt, np.float32),
        "A_log": np.asarray(L0_attn_A_log, np.float32),
        "B_mat": np.asarray(L0_attn_B_mat, np.float32),
        "C_mat": np.asarray(L0_attn_C_mat, np.float32),
        "D_vec": np.asarray(L0_attn_D_vec, np.float32),
        "wr": np.asarray(L0_ffn_wr, np.float32),
        "rb": np.asarray(L0_ffn_rb, np.float32),
        "sg": np.asarray(L0_ffn_sg, np.float32),
        "su": np.asarray(L0_ffn_su, np.float32),
        "sd": np.asarray(L0_ffn_sd, np.float32),
        "rg": np.asarray(L0_ffn_rg, np.float32),
        "ru": np.asarray(L0_ffn_ru, np.float32),
        "rd": np.asarray(L0_ffn_rd, np.float32),
    })
    nc = build_nc(cfg)
    in_maps = []
    for c in range(n_cores):
        xc = x[cfg.BPC * c: cfg.BPC * (c + 1)].reshape(cfg.BPC * T, H)
        m = {"x": np.ascontiguousarray(xc)}
        m.update(w)
        in_maps.append(m)
    res = run_bass_kernel_spmd(nc, in_maps, core_ids=list(range(n_cores)))
    out = np.concatenate(
        [r["out"].reshape(cfg.BPC, T, H) for r in res.results], axis=0)
    return out.astype(np.float32)


# version 34
# speedup vs baseline: 1.3663x; 1.2051x over previous
import contextlib
import numpy as np

import concourse.bass as bass
import concourse.mybir as mybir
from concourse.bass_utils import run_bass_kernel_spmd

F32 = mybir.dt.float32
BF16 = mybir.dt.bfloat16
AF = mybir.ActivationFunctionType
ALU = mybir.AluOpType
AX = mybir.AxisListType


class Cfg:
    def __init__(self, H=1024, N=16, I=2816, ER=7, T=2048, BPC=2, mini=False):
        self.H, self.N, self.I, self.ER, self.T, self.BPC = H, N, I, ER, T, BPC
        self.E = ER + 1
        self.G = H // 128
        self.KI = I // 128
        assert self.KI % 2 == 0 or True
        self.IH = (self.KI + 1) // 2
        self.IH2 = self.KI - self.IH
        self.UT = min(1024, T)
        self.NT = self.UT // 128
        self.UPB = T // self.UT
        self.UNITS = BPC * self.UPB
        self.CH = min(512, self.UT)
        self.NCH = self.UT // self.CH
        self.mini = mini



BODIES = ("tensor", "scalar", "vector", "gpsimd", "sync")


class Prog:
    def __init__(self, nc, stack):
        self.nc = nc
        self.streams = {}
        self.body_of = {}
        self.cnt = {}
        names = [("pe", "tensor"), ("act", "scalar"), ("dve", "vector"),
                 ("pool", "gpsimd")]
        names += [(f"sd{i}", "sync") for i in range(self.NSD)]
        names += [(f"gd{i}", "gpsimd") for i in range(self.NGD)]
        for s, body in names:
            self.streams[s] = stack.enter_context(nc.semaphore(f"sem_{s}"))
            self.body_of[s] = body
            self.cnt[s] = 0
        self.vc = {b: {} for b in BODIES}
        self.snap = {}
        self.last_write = {}
        self.readers = {}
        self.body_ops = {b: [] for b in BODIES}
        self._sd_rr = 0
        self._gd_rr = 0

    NSD = 5
    NGD = 6
    DMA_STREAMS = frozenset({f"sd{i}" for i in range(NSD)} |
                            {f"gd{i}" for i in range(NGD)})

    def sd_queue(self):
        self._sd_rr += 1
        return "sd%d" % (self._sd_rr % self.NSD)

    def gd_queue(self):
        self._gd_rr += 1
        return "gd%d" % (self._gd_rr % self.NGD)

    def op(self, stream, reads, writes, emit, inc=None):
        body = self.body_of[stream]
        if inc is None:
            inc = 16 if stream in self.DMA_STREAMS else 1
        if stream in self.DMA_STREAMS:
            reads = list(reads) + [("_dmaq", stream)]
            writes = list(writes) + [("_dmaq", stream)]
        needs = {}
        for k in reads:
            lw = self.last_write.get(k)
            if lw is not None:
                s2, v2 = lw
                if needs.get(s2, 0) < v2:
                    needs[s2] = v2
        for k in writes:
            lw = self.last_write.get(k)
            if lw is not None:
                s2, v2 = lw
                if needs.get(s2, 0) < v2:
                    needs[s2] = v2
            for (s2, v2) in self.readers.get(k, ()):
                if needs.get(s2, 0) < v2:
                    needs[s2] = v2
        vc = self.vc[body]
        if stream == "pe" and "pe" in needs:
            del needs["pe"]
        waits = []
        for s2 in sorted(needs):
            v2 = needs[s2]
            if vc.get(s2, 0) < v2:
                waits.append((s2, v2))
                vc[s2] = v2
                sn = self.snap.get((s2, v2))
                if sn:
                    for s3, v3 in sn.items():
                        if vc.get(s3, 0) < v3:
                            vc[s3] = v3
        self.cnt[stream] += inc
        val = self.cnt[stream]
        self.snap[(stream, val)] = dict(vc, **{stream: val})
        for k in writes:
            self.last_write[k] = (stream, val)
            self.readers[k] = []
        for k in reads:
            self.readers.setdefault(k, []).append((stream, val))
        self.body_ops[body].append((waits, emit, stream, inc))

    def emit_all(self, block, final_stores):
        sems = self.streams
        body_ops = self.body_ops

        def run(eng, body):
            for waits, emit, stream, inc in body_ops[body]:
                for s2, v2 in waits:
                    eng.wait_ge(sems[s2], v2)
                inst = emit(eng)
                if inst is not None:
                    inst.then_inc(sems[stream], inc)
            if body == "sync":
                for s2, v2 in final_stores:
                    eng.wait_ge(sems[s2], v2)

        @block.tensor
        def _(eng):
            run(eng, "tensor")

        @block.scalar
        def _(eng):
            run(eng, "scalar")

        @block.vector
        def _(eng):
            run(eng, "vector")

        @block.gpsimd
        def _(eng):
            run(eng, "gpsimd")

        @block.sync
        def _(eng):
            run(eng, "sync")



def build_nc(cfg: Cfg):
    nc = bass.Bass("TRN2", target_bir_lowering=False, debug=False)
    H, N, I, E, G, KI = cfg.H, cfg.N, cfg.I, cfg.E, cfg.G, cfg.KI
    UT, NT, CH, NCH, UNITS = cfg.UT, cfg.NT, cfg.CH, cfg.NCH, cfg.UNITS
    TOK = cfg.BPC * cfg.T

    x_hbm = nc.dram_tensor("x", [TOK, H], F32, kind="ExternalInput").ap()
    out_hbm = nc.dram_tensor("out", [TOK, H], F32, kind="ExternalOutput").ap()
    wdt_hbm = nc.dram_tensor("wdtT", [128, G * H], F32, kind="ExternalInput").ap()
    bdt_hbm = nc.dram_tensor("bdt", [128, G], F32, kind="ExternalInput").ap()
    alog_hbm = nc.dram_tensor("alog", [128, G * N], F32, kind="ExternalInput").ap()
    bmat_hbm = nc.dram_tensor("bmat", [128, G * N], F32, kind="ExternalInput").ap()
    cmat_hbm = nc.dram_tensor("cmat", [128, G * N], F32, kind="ExternalInput").ap()
    dvec_hbm = nc.dram_tensor("dvec", [128, G], F32, kind="ExternalInput").ap()
    wr_hbm = nc.dram_tensor("wrT", [128, G * (E - 1)], F32,
                            kind="ExternalInput").ap()
    rb_hbm = nc.dram_tensor("rb", [E - 1], F32, kind="ExternalInput").ap()
    wgu_hbm, d_hbm = [], []
    for e in range(E):
        wgu_hbm.append(nc.dram_tensor(f"wgu{e}", [KI * 128, G * 256], F32,
                                      kind="ExternalInput").ap())
        d_hbm.append(nc.dram_tensor(f"dT{e}", [I * H], F32,
                                    kind="ExternalInput").ap())
    wgu_bf, d_bf = [], []
    for e in range(E):
        wgu_bf.append(nc.dram_tensor(f"wgubf{e}", [KI * 128, G * 256], BF16,
                                     kind="Internal").ap())
        d_bf.append(nc.dram_tensor(f"dbf{e}", [I * H], BF16,
                                   kind="Internal").ap())

    stack = contextlib.ExitStack()
    sb = lambda name, shape, dt: stack.enter_context(nc.sbuf_tensor(name, shape, dt))
    psb = lambda name, shape, dt: stack.enter_context(nc.psum_tensor(name, shape, dt))

    xnT = sb("xnT", [128, G, UT], BF16)
    act = sb("act", [128, cfg.IH, UT], BF16)
    acc = sb("acc", [128, NT, H], F32)
    ytok = sb("ytok", [128, NT, H], BF16)
    dtbf = sb("dtbf", [128, G, UT], BF16)
    wdtT = sb("wdtT_sb", [128, G, H], BF16)
    du = [sb(f"du{i}", [128, UT], F32) for i in range(2)]
    NAB = 3
    abuf = [sb(f"abuf{i}", [128, CH], F32) for i in range(NAB)]
    bbuf = [sb(f"bbuf{i}", [128, CH], F32) for i in range(2)]
    sbuf_s = [sb(f"sbuf{i}", [128, CH], F32) for i in range(2)]
    ysl = [sb(f"ysl{i}", [128, UT], F32) for i in range(2)]
    tsb = [sb(f"tsb{i}", [128, CH], F32) for i in range(2)]
    WGB = min(128, I)
    NWB = 3
    wgu = [sb(f"wgub{i}", [128, G, 256], BF16) for i in range(NWB)]
    WDB = min(4, KI)
    NWD = 4
    wd = [sb(f"wd{i}", [128, WDB, 512 if H >= 1024 else H], BF16)
          for i in range(NWD)]
    xns = [sb(f"xns{i}", [128, H], BF16) for i in range(2)]
    xres = [sb(f"xres{i}", [128, min(512, H)], F32) for i in range(2)]
    xb16 = sb("xb16", [128, NT, H], BF16)
    stats = sb("stats", [128, 64], F32)
    rt = sb("rt", [128, 4 * (E - 1)], F32)
    rsc = sb("rsc", [128, 16], F32)
    wtok = sb("wtok", [128, NT, E - 1], F32)
    a_neg = sb("a_neg", [128, G, N], F32)
    cb = sb("cb", [128, G, N], F32)
    dv = sb("dv", [128, G], F32)
    bdt_sb = sb("bdt_sb", [128, G], F32)
    carry = sb("carry", [128, G, N], F32)
    wrT_sb = sb("wrT_sb", [128, G, E - 1], BF16)
    rb_row = sb("rb_row", [1, E - 1], F32)
    ones_row = sb("ones_row", [1, 128], F32)
    ident_bf = sb("ident_bf", [128, 128], BF16)
    ident_f = sb("ident_f", [128, 128], F32)
    ones_col = sb("ones_col", [128, 1], F32)

    ps_gate = [psb(f"ps_gate{i}", [128, CH], F32) for i in range(2)]
    ps_up = [psb(f"ps_up{i}", [128, CH], F32) for i in range(2)]
    ps_dn = [psb(f"ps_dn{i}", [128, min(512, H)], F32) for i in range(2)]
    GROT = [ps_gate[0], ps_gate[1]]
    GROTK = [("ps_gate", 0), ("ps_gate", 1)]
    UROT = [ps_up[0], ps_up[1]]
    UROTK = [("ps_up", 0), ("ps_up", 1)]
    DROT = [ps_dn[0], ps_dn[1], ps_up[0], ps_up[1], ps_gate[0], ps_gate[1]]
    DROTK = [("ps_dn", 0), ("ps_dn", 1), ("ps_up", 0), ("ps_up", 1),
             ("ps_gate", 0), ("ps_gate", 1)]
    ps_m32 = psb("ps_m32", [128, 512], F32)
    ps_mbf = psb("ps_mbf", [128, 1024], BF16)

    p = Prog(nc, stack)
    K_CONST = ("const",)

    def dma(stream_pick, dst_ap, src_ap, reads, writes, slow=False):
        q = stream_pick()
        if slow:
            p.op(q, reads, writes,
                 (lambda e, d=dst_ap, s=src_ap: e.dma_start(
                     out=d, in_=s, allow_slow_non_contiguous=True)))
        else:
            p.op(q, reads, writes,
                 (lambda e, d=dst_ap, s=src_ap: e.dma_start(out=d, in_=s)))

    for i in range(NT):
        dma(p.gd_queue, xb16[:, i, :], x_hbm[128 * i:128 * (i + 1), :],
            [], [("xb16", i)])

    p.op("pool", [], [K_CONST + ("idb",)], lambda e: e.memset(ident_bf[:], 0.0))
    p.op("pool", [K_CONST + ("idb",)], [K_CONST + ("idb",)],
         lambda e: e.affine_select(out=ident_bf[:], in_=ident_bf[:],
                                   compare_op=ALU.not_equal, fill=1.0, base=0,
                                   pattern=[[-1, 128]], channel_multiplier=1))
    p.op("pool", [], [K_CONST + ("idf",)], lambda e: e.memset(ident_f[:], 0.0))
    p.op("pool", [K_CONST + ("idf",)], [K_CONST + ("idf",)],
         lambda e: e.affine_select(out=ident_f[:], in_=ident_f[:],
                                   compare_op=ALU.not_equal, fill=1.0, base=0,
                                   pattern=[[-1, 128]], channel_multiplier=1))
    p.op("pool", [], [K_CONST + ("ones",)], lambda e: e.memset(ones_col[:], 1.0))

    dma(p.gd_queue, wdtT[:],
        wdt_hbm.rearrange("p (kk g) -> p kk g", kk=G),
        [], [("wdtT",)])
    dma(p.gd_queue, a_neg[:], alog_hbm.rearrange("p (g n) -> p g n", g=G),
        [], [K_CONST + ("alog",)])
    dma(p.gd_queue, cb[:], bmat_hbm.rearrange("p (g n) -> p g n", g=G),
        [], [K_CONST + ("cbB",)])
    dma(p.gd_queue, ysl[0][:, 0:G * N].rearrange("p (g n) -> p g n", g=G),
        cmat_hbm.rearrange("p (g n) -> p g n", g=G),
        [], [("ysl", 0)])
    dma(p.gd_queue, dv[:], dvec_hbm, [], [K_CONST + ("dv",)])
    dma(p.gd_queue, bdt_sb[:], bdt_hbm, [], [K_CONST + ("bdt",)])
    dma(p.gd_queue, wrT_sb[:], wr_hbm.rearrange("p (kk e) -> p kk e", kk=G),
        [], [K_CONST + ("wrT",)])
    dma(p.gd_queue, rb_row[:], rb_hbm[None, :], [], [K_CONST + ("rb",)])
    p.op("pool", [], [K_CONST + ("ones_row",)],
         lambda e: e.memset(ones_row[:], 1.0))
    p.op("act", [K_CONST + ("alog",)], [K_CONST + ("alog",)],
         lambda e: e.activation(out=a_neg[:], in_=a_neg[:], func=AF.Exp))
    p.op("dve", [K_CONST + ("alog",)], [K_CONST + ("A",)],
         lambda e: e.tensor_scalar(out=a_neg[:], in0=a_neg[:], scalar1=-1.0,
                                   scalar2=None, op0=ALU.mult))
    p.op("dve", [K_CONST + ("cbB",), ("ysl", 0)], [K_CONST + ("cb",)],
         lambda e: e.tensor_tensor(
             out=cb[:], in0=cb[:],
             in1=ysl[0][:, 0:G * N].rearrange("p (g n) -> p g n", g=G),
             op=ALU.mult))

    def ln_steps_a(u, i, xrow0):
        bi = i % 2
        kx = ("xb16", i)
        kst = ("st", i)
        so = 8 * i
        st = []
        if u > 0:
            st.append(lambda: dma(p.gd_queue, xb16[:, i, :],
                                  x_hbm[xrow0:xrow0 + 128, :], [], [kx]))
        st.append(lambda: p.op("act", [kx], [("xns", bi), kst],
             lambda e: e.activation(
                 out=xns[bi][:], in_=xb16[:, i, :], func=AF.Identity,
                 accum_out=stats[:, so + 0:so + 1])))
        st.append(lambda: p.op("act", [kst], [kst],
             lambda e: e.mul(stats[:, so + 1:so + 2],
                             stats[:, so + 0:so + 1], -1.0 / H)))
        st.append(lambda: p.op("act", [kx, kst], [("xns", bi), kst],
             lambda e: e.activation(
                 out=xns[bi][:], in_=xb16[:, i, :], func=AF.Square,
                 bias=stats[:, so + 1:so + 2],
                 accum_out=stats[:, so + 2:so + 3])))
        st.append(lambda: p.op("dve", [kst], [kst],
             lambda e: e.tensor_scalar(
                 out=stats[:, so + 3:so + 4], in0=stats[:, so + 2:so + 3],
                 scalar1=1.0 / H, scalar2=1e-5, op0=ALU.mult, op1=ALU.add)))
        st.append(lambda: p.op("act", [kst], [kst],
             lambda e: e.activation(
                 out=stats[:, so + 4:so + 5], in_=stats[:, so + 3:so + 4],
                 func=AF.Ln)))
        st.append(lambda: p.op("act", [kst], [kst],
             lambda e: e.activation(
                 out=stats[:, so + 5:so + 6], in_=stats[:, so + 4:so + 5],
                 func=AF.Exp, scale=-0.5)))
        st.append(lambda: p.op("dve", [kst], [kst],
             lambda e: e.tensor_tensor(
                 out=stats[:, so + 6:so + 7], in0=stats[:, so + 1:so + 2],
                 in1=stats[:, so + 5:so + 6], op=ALU.mult)))
        return st

    def ln_steps_b(u, i, xrow0):
        bi = i % 2
        kst = ("st", i)
        so = 8 * i
        st = []
        st.append(lambda: p.op("act", [("xb16", i), kst], [("xns", bi)],
             lambda e: e.activation(
                 out=xns[bi][:], in_=xb16[:, i, :], func=AF.Identity,
                 scale=stats[:, so + 5:so + 6], bias=stats[:, so + 6:so + 7])))

        def tr(e):
            last = None
            for j in range(G):
                last = e.transpose(ps_mbf[:, 128 * j:128 * (j + 1)],
                                   xns[bi][:, 128 * j:128 * (j + 1)],
                                   ident_bf[:])
            return last
        st.append(lambda: p.op("pe", [("xns", bi), K_CONST + ("idb",)],
                               [("ps_mbf",)], tr))
        st.append(lambda: p.op("dve", [("ps_mbf",)], [("xnT", i)],
             lambda e: e.tensor_copy(
                 out=xnT[:, :, 128 * i:128 * (i + 1)],
                 in_=ps_mbf[:, 0:128 * G].rearrange("p (g c) -> p g c", g=G))))
        return st

    def router_steps(u, i):
        bi = i % 2
        ER = E - 1
        ko = ("rt", bi)
        krs = ("rsc", bi)
        l_sl = rt[:, bi * 2 * ER: bi * 2 * ER + ER]
        e_sl = rt[:, bi * 2 * ER + ER: bi * 2 * ER + 2 * ER]
        r0 = bi * 8
        st = []

        def mm(e):
            for kk in range(G):
                e.matmul(ps_m32[:, 0:ER],
                         xnT[:, kk, 128 * i:128 * (i + 1)],
                         wrT_sb[:, kk, :],
                         start=(kk == 0), stop=False)
            return e.matmul(ps_m32[:, 0:ER], ones_row[:], rb_row[:],
                            start=False, stop=True)
        st.append(lambda: p.op("pe", [("xnT", i), K_CONST + ("wrT",),
                                      K_CONST + ("rb",),
                                      K_CONST + ("ones_row",)],
                               [("ps_m32",)], mm))
        st.append(lambda: p.op("dve", [("ps_m32",)], [ko],
             lambda e: e.tensor_copy(out=l_sl, in_=ps_m32[:, 0:E - 1])))
        st.append(lambda: p.op("dve", [ko], [krs],
             lambda e: e.tensor_reduce(
                 out=rsc[:, r0:r0 + 1], in_=l_sl, axis=AX.X, op=ALU.max)))
        st.append(lambda: p.op("dve", [ko, krs], [ko],
             lambda e: e.tensor_scalar(
                 out=e_sl, in0=l_sl, scalar1=rsc[:, r0:r0 + 1], scalar2=None,
                 op0=ALU.subtract)))
        st.append(lambda: p.op("act", [ko], [ko],
             lambda e: e.activation(out=e_sl, in_=e_sl, func=AF.Exp)))
        st.append(lambda: p.op("dve", [ko], [ko],
             lambda e: e.tensor_scalar(
                 out=l_sl, in0=e_sl, scalar1=1.0, scalar2=None,
                 op0=ALU.is_lt)))
        st.append(lambda: p.op("dve", [ko], [ko],
             lambda e: e.tensor_tensor(
                 out=l_sl, in0=l_sl, in1=e_sl, op=ALU.mult)))
        st.append(lambda: p.op("dve", [ko], [krs],
             lambda e: e.tensor_reduce(
                 out=rsc[:, r0 + 1:r0 + 2], in_=l_sl, axis=AX.X, op=ALU.max)))
        st.append(lambda: p.op("dve", [krs], [krs],
             lambda e: e.tensor_scalar(
                 out=rsc[:, r0 + 2:r0 + 3], in0=rsc[:, r0 + 1:r0 + 2],
                 scalar1=1.0, scalar2=None, op0=ALU.add)))
        st.append(lambda: p.op("dve", [krs], [krs],
             lambda e: e.reciprocal(out=rsc[:, r0 + 3:r0 + 4],
                                    in_=rsc[:, r0 + 2:r0 + 3])))
        st.append(lambda: p.op("dve", [ko, krs], [("wtok", i)],
             lambda e: e.tensor_scalar(
                 out=wtok[:, i, :], in0=e_sl, scalar1=rsc[:, r0 + 1:r0 + 2],
                 scalar2=None, op0=ALU.is_ge)))
        st.append(lambda: p.op("dve", [ko, ("wtok", i)], [("wtok", i)],
             lambda e: e.tensor_tensor(
                 out=wtok[:, i, :], in0=wtok[:, i, :], in1=e_sl,
                 op=ALU.mult)))
        st.append(lambda: p.op("dve", [krs, ("wtok", i)], [("wtok", i)],
             lambda e: e.tensor_scalar(
                 out=wtok[:, i, :], in0=wtok[:, i, :],
                 scalar1=rsc[:, r0 + 3:r0 + 4], scalar2=None, op0=ALU.mult)))
        return st

    def dt_slab(u, g):
        for c in range(NCH):
            xnT_keys = [("xnT", i) for i in range(NT)
                        if CH * c <= 128 * i < CH * (c + 1)]
            pk = ("ps_dn", c % 2)

            def mm(e, g=g, c=c, pk=pk):
                last = None
                for kk in range(G):
                    last = e.matmul(ps_dn[c % 2][:, 0:CH],
                                    wdtT[:, kk, 128 * g:128 * (g + 1)],
                                    xnT[:, kk, CH * c:CH * (c + 1)],
                                    start=(kk == 0), stop=(kk == G - 1))
                return last
            p.op("pe", xnT_keys + [("wdtT",)], [pk], mm)
            ka = ("a", (2 * g + c) % NAB)
            p.op("act", [pk, K_CONST + ("bdt",)], [ka],
                 lambda e, g=g, c=c, ka=ka: e.activation(
                     out=abuf[ka[1]][:], in_=ps_dn[c % 2][:, 0:CH], func=AF.Exp,
                     bias=bdt_sb[:, g:g + 1]))
            p.op("act", [ka, K_CONST + ("ones",)], [("dtbf", g, c)],
                 lambda e, g=g, c=c, ka=ka: e.activation(
                     out=dtbf[:, g, CH * c:CH * (c + 1)], in_=abuf[ka[1]][:],
                     func=AF.Ln, bias=ones_col[:]))

    def scan_slab_thunks(u, g, first_unit_of_batch):
        gb = g % 2
        kdu = ("du", gb)
        kysl = ("ysl", gb)
        thunks = []

        def t_init():
            p.op("dve", [("dtbf", g, 0), ("dtbf", g, min(1, NCH - 1))] +
                 [("xnT", i) for i in range(NT)], [kdu],
                 lambda e, g=g, gb=gb: e.tensor_tensor(
                     out=du[gb][:], in0=dtbf[:, g, :], in1=xnT[:, g, :],
                     op=ALU.mult))
            for c in range(NCH):
                p.op("act", [("xnT", i) for i in range(NT)] +
                     [K_CONST + ("dv",)], [kysl],
                     lambda e, g=g, c=c, gb=gb: e.activation(
                         out=ysl[gb][:, CH * c:CH * (c + 1)],
                         in_=xnT[:, g, CH * c:CH * (c + 1)], func=AF.Identity,
                         scale=dv[:, g:g + 1]))
            if first_unit_of_batch:
                p.op("dve", [], [("carry", g)],
                     lambda e, g=g: e.memset(carry[:, g, :], 0.0))
        thunks.append(t_init)

        def t_iter(n, c):
            ia = (n * NCH + c) % NAB
            ka = ("a", ia)
            p.op("act", [("dtbf", g, c), K_CONST + ("A",)], [ka],
                 lambda e, g=g, c=c, n=n, ia=ia: e.activation(
                     out=abuf[ia][:], in_=dtbf[:, g, CH * c:CH * (c + 1)],
                     func=AF.Exp, scale=a_neg[:, g, n:n + 1]))
            ib2 = (n * NCH + c) % 2
            kb = ("b", ib2)
            p.op("act", [kdu, K_CONST + ("cb",)], [kb],
                 lambda e, ib2=ib2, g=g, n=n, c=c, gb=gb: e.activation(
                     out=bbuf[ib2][:], in_=du[gb][:, CH * c:CH * (c + 1)],
                     func=AF.Copy, scale=cb[:, g, n:n + 1]))
            isb = (n * NCH + c) % 2
            ks = ("s", isb)
            if c == 0:
                init_ap = carry[:, g, n:n + 1]
                init_key = ("carry", g)
            else:
                prev = (n * NCH + c - 1) % 2
                init_ap = sbuf_s[prev][:, CH - 1:CH]
                init_key = ("s", prev)
            p.op("dve", [ka, kb, init_key], [ks],
                 lambda e, ia=ia, ib2=ib2, isb=isb, init_ap=init_ap:
                 e.tensor_tensor_scan(
                     out=sbuf_s[isb][:], data0=abuf[ia][:],
                     data1=bbuf[ib2][:],
                     initial=init_ap, op0=ALU.mult, op1=ALU.add))
            p.op("dve", [ks, kysl], [kysl],
                 lambda e, isb=isb, c=c, gb=gb: e.tensor_tensor(
                     out=ysl[gb][:, CH * c:CH * (c + 1)],
                     in0=ysl[gb][:, CH * c:CH * (c + 1)],
                     in1=sbuf_s[isb][:], op=ALU.add))
            if c == NCH - 1:
                p.op("dve", [ks], [("carry", g)],
                     lambda e, isb=isb, g=g, n=n: e.tensor_copy(
                         out=carry[:, g, n:n + 1],
                         in_=sbuf_s[isb][:, CH - 1:CH]))
        for n in range(N):
            for c in range(NCH):
                thunks.append(lambda n=n, c=c: t_iter(n, c))

        def t_tr(j0, j1):
            def tr(e, j0=j0, j1=j1, gb=gb):
                last = None
                for j in range(j0, j1):
                    last = e.transpose(
                        ps_m32[:, 128 * (j - j0):128 * (j - j0 + 1)],
                        ysl[gb][:, 128 * j:128 * (j + 1)], ident_f[:])
                return last
            p.op("pe", [kysl, K_CONST + ("idf",)], [("ps_m32",)], tr)
            p.op("dve", [("ps_m32",)],
                 [("ytok", j, g) for j in range(j0, j1)],
                 lambda e, j0=j0, j1=j1, g=g: e.tensor_copy(
                     out=ytok[:, j0:j1, 128 * g:128 * (g + 1)],
                     in_=ps_m32[:, 0:128 * (j1 - j0)].rearrange(
                         "p (a b) -> p a b", a=j1 - j0)))
        nblk = UT // 128
        tr_thunks = []
        for half in range((nblk + 3) // 4):
            j0, j1 = 4 * half, min(4 * half + 4, nblk)
            tr_thunks.append(lambda j0=j0, j1=j1: t_tr(j0, j1))
        return thunks, tr_thunks

    def gateup_blocks(it0, it1):
        blocks = []
        col = it0 * 128
        while col < it1 * 128:
            bw = min(WGB, it1 * 128 - col)
            blocks.append((col, bw))
            col += bw
        return blocks

    class WeightWheel:

        def __init__(self, items, ahead=3):
            self.items = items
            self.ahead = ahead
            self.emitted = 0
            self.consumed = 0

        def _pump(self, target):
            while self.emitted < min(len(self.items), target):
                u_i, e_i, col, bw = self.items[self.emitted]
                seq = self.emitted
                b = col // 128
                if u_i == 0:
                    dma(p.gd_queue, wgu[seq % NWB][:],
                        wgu_hbm[e_i][128 * b:128 * (b + 1), :]
                        .rearrange("p (kk c) -> p kk c", kk=G),
                        [], [("wgu", seq % NWB)])
                    dma(p.sd_queue,
                        wgu_bf[e_i][128 * b:128 * (b + 1), :]
                        .rearrange("p (kk c) -> p kk c", kk=G),
                        wgu[seq % NWB][:],
                        [("wgu", seq % NWB)], [("gbf", e_i, b)])
                else:
                    dma(p.sd_queue, wgu[seq % NWB][:],
                        wgu_bf[e_i][128 * b:128 * (b + 1), :]
                        .rearrange("p (kk c) -> p kk c", kk=G),
                        [("gbf", e_i, b)], [("wgu", seq % NWB)])
                self.emitted += 1

        def ensure_ahead(self):
            s = self.consumed
            self.consumed = s + 1
            self._pump(s + self.ahead)
            return s

        def prefetch(self):
            self._pump(self.consumed + self.ahead)

    def _wd_offsets():
        HH0 = min(512, H)
        off = {}
        pos = 0
        for hidx, (it0, it1) in enumerate([(0, cfg.IH), (cfg.IH, KI)]):
            for hh in range(H // HH0):
                kt_ = it0
                bidx = 0
                while kt_ < it1:
                    bk = min(WDB, it1 - kt_)
                    off[(hidx, hh, bidx)] = (pos, kt_, bk)
                    pos += 128 * bk * HH0
                    kt_ += bk
                    bidx += 1
        return off
    WD_OFF = _wd_offsets()

    def ffn_expert(u, e_idx, drain, wheel, emit_combine=False):
        dhbm = d_hbm[e_idx]
        xnT_chunk_keys = [[("xnT", i) for i in range(NT)
                           if CH * c <= 128 * i < CH * (c + 1)]
                          for c in range(NCH)]
        halves = [(0, cfg.IH), (cfg.IH, KI)]
        for hidx, (it0, it1) in enumerate(halves):
            if it1 <= it0:
                continue
            ntile_half = it1 - it0
            HH0 = min(512, H)
            dblocks_by_hh = {}
            for hh in range(H // HH0):
                dblocks = []
                kt_ = it0
                while kt_ < it1:
                    bk = min(WDB, it1 - kt_)
                    dblocks.append((kt_, bk))
                    kt_ += bk
                dblocks_by_hh[hh] = dblocks

            nblk_hh = len(dblocks_by_hh[0])

            def emit_wd_dmas(hh, only=None):
                for bidx, (kt0, bk) in enumerate(dblocks_by_hh[hh]):
                    wdi = (nblk_hh * hh + bidx) % NWD
                    if only is not None and wdi not in only:
                        continue
                    pos, _kt0, _bk = WD_OFF[(hidx, hh, bidx)]
                    kbf = ("dbf", e_idx, hidx, hh, bidx)
                    if u == 0:
                        dma(p.gd_queue, wd[wdi][:, 0:bk, 0:HH0],
                            dhbm[pos:pos + 128 * bk * HH0]
                            .rearrange("(p kk h) -> p kk h", p=128, kk=bk),
                            [], [("wd", wdi)])
                        dma(p.sd_queue,
                            d_bf[e_idx][pos:pos + 128 * bk * HH0]
                            .rearrange("(p kk h) -> p kk h", p=128, kk=bk),
                            wd[wdi][:, 0:bk, 0:HH0],
                            [("wd", wdi)], [kbf])
                    else:
                        dma(p.sd_queue, wd[wdi][:, 0:bk, 0:HH0],
                            d_bf[e_idx][pos:pos + 128 * bk * HH0]
                            .rearrange("(p kk h) -> p kk h", p=128, kk=bk),
                            [kbf], [("wd", wdi)])
            emit_wd_dmas(0)
            blocks = gateup_blocks(it0, it1)
            for bidx, (col, bw) in enumerate(blocks):
                seq = wheel.ensure_ahead()
                kwg = ("wgu", seq % NWB)
                kwu = kwg
                for sub in range(bw // 128):
                    it = (col + 128 * sub) // 128
                    lt = it - it0
                    for c in range(NCH):
                        ri = (it * NCH + c) % len(GROT)
                        pg = GROTK[ri]
                        pu = UROTK[ri]

                        def mmg(e, sub=sub, c=c, ri=ri, seq=seq):
                            last = None
                            for kk in range(G):
                                last = e.matmul(
                                    GROT[ri][:],
                                    wgu[seq % NWB][:, kk, 0:128],
                                    xnT[:, kk, CH * c:CH * (c + 1)],
                                    start=(kk == 0), stop=(kk == G - 1))
                            return last
                        p.op("pe", [kwg] + xnT_chunk_keys[c], [pg], mmg)

                        def mmu(e, sub=sub, c=c, ri=ri, seq=seq):
                            last = None
                            for kk in range(G):
                                last = e.matmul(
                                    UROT[ri][:],
                                    wgu[seq % NWB][:, kk, 128:256],
                                    xnT[:, kk, CH * c:CH * (c + 1)],
                                    start=(kk == 0), stop=(kk == G - 1))
                            return last
                        p.op("pe", [kwu] + xnT_chunk_keys[c], [pu], mmu)
                        kt = ("tsb", (it * NCH + c) % 2)
                        p.op("act", [pg], [kt],
                             lambda e, it=it, c=c, ri=ri: e.activation(
                                 out=tsb[(it * NCH + c) % 2][:],
                                 in_=GROT[ri][:],
                                 func=AF.Tanh, scale=0.5))
                        p.op("dve", [kt, pg], [kt],
                             lambda e, it=it, c=c, ri=ri: e.scalar_tensor_tensor(
                                 out=tsb[(it * NCH + c) % 2][:],
                                 in0=tsb[(it * NCH + c) % 2][:], scalar=1.0,
                                 in1=GROT[ri][:],
                                 op0=ALU.add, op1=ALU.mult))
                        p.op("dve", [kt, pu], [("act", lt, c)],
                             lambda e, it=it, c=c, lt=lt, ri=ri:
                             e.scalar_tensor_tensor(
                                 out=act[:, lt, CH * c:CH * (c + 1)],
                                 in0=tsb[(it * NCH + c) % 2][:], scalar=0.5,
                                 in1=UROT[ri][:],
                                 op0=ALU.mult, op1=ALU.mult))
                        drain(1)
            HH = min(512, H)
            nhh = H // HH
            def finish_tg(tg, hh):
                di = tg % min(4, NT, len(DROT))
                pk = DROTK[di]
                kacc = ("acc", tg, hh)
                if e_idx == 0 and hidx == 0:
                    p.op("dve", [pk], [kacc],
                         lambda e, tg=tg, hh=hh, di=di: e.tensor_copy(
                             out=acc[:, tg, HH * hh:HH * (hh + 1)],
                             in_=DROT[di][:, 0:HH]))
                elif e_idx == 0:
                    p.op("dve", [pk, kacc], [kacc],
                         lambda e, tg=tg, hh=hh, di=di: e.tensor_tensor(
                             out=acc[:, tg, HH * hh:HH * (hh + 1)],
                             in0=acc[:, tg, HH * hh:HH * (hh + 1)],
                             in1=DROT[di][:, 0:HH], op=ALU.add))
                else:
                    p.op("dve", [pk, kacc, ("wtok", tg)], [kacc],
                         lambda e, tg=tg, hh=hh, e_idx=e_idx, di=di:
                         e.scalar_tensor_tensor(
                             out=acc[:, tg, HH * hh:HH * (hh + 1)],
                             in0=DROT[di][:, 0:HH],
                             scalar=wtok[:, tg, e_idx - 1:e_idx],
                             in1=acc[:, tg, HH * hh:HH * (hh + 1)],
                             op0=ALU.mult, op1=ALU.add))

            if emit_combine and it1 == KI:
                drain(len(pending))
            for hh in range(nhh):
                wheel.prefetch()
                cur_bufs = {(nblk_hh * hh + b) % NWD
                            for b in range(len(dblocks_by_hh[hh]))}
                nxt_bufs = {(nblk_hh * (hh + 1) + b) % NWD
                            for b in range(len(dblocks_by_hh[hh]))}
                if hh + 1 < nhh:
                    emit_wd_dmas(hh + 1, only=nxt_bufs - cur_bufs)
                dblocks = dblocks_by_hh[hh]
                WAVE = min(4, NT, len(DROT))

                def emit_dblock(tg, bidx, di):
                    kt0, bk = dblocks[bidx]
                    pk = DROTK[di]

                    def mmd(e, tg=tg, hh=hh, it0=it0, it1=it1,
                            kt0=kt0, bk=bk, bidx=bidx, di=di):
                        last = None
                        for kk in range(bk):
                            kt_g = kt0 + kk
                            last = e.matmul(
                                DROT[di][:, 0:HH],
                                act[:, kt_g - it0, 128 * tg:128 * (tg + 1)],
                                wd[(nblk_hh * hh + bidx) % NWD][:, kk, 0:HH],
                                start=(kt_g == it0), stop=(kt_g == it1 - 1))
                        return last
                    blk_act_keys = [("act", kt0 - it0 + kk, c)
                                    for kk in range(bk) for c in range(NCH)]
                    p.op("pe", blk_act_keys +
                         [("wd", (nblk_hh * hh + bidx) % NWD), pk], [pk], mmd)

                for w0 in range(0, NT, WAVE):
                    wave = range(w0, min(w0 + WAVE, NT))
                    for tg in wave:
                        for bidx in range(len(dblocks) - 1):
                            emit_dblock(tg, bidx, tg % WAVE)
                    for tg in wave:
                        emit_dblock(tg, len(dblocks) - 1, tg % WAVE)
                        finish_tg(tg, hh)
                        drain(1)
                        if emit_combine and it1 == KI:
                            d_tile_calc(u, tg, hh, u * UT + 128 * tg,
                                        hh * NT + tg)
                if hh + 1 < nhh:
                    emit_wd_dmas(hh + 1, only=nxt_bufs & cur_bufs)

    def d_tile_thunks(u, i, xrow0):
        return [lambda: d_tile_calc(u, i, 0, xrow0, 2 * i),
                (lambda: d_tile_calc(u, i, 1, xrow0, 2 * i + 1))
                if H > 512 else None]

    D_STG = [(sbuf_s[0], ("s", 0)), (sbuf_s[1], ("s", 1)),
             (tsb[0], ("tsb", 0)), (tsb[1], ("tsb", 1))]

    def d_tile_calc(u, i, hh, xrow0, r):
        HH = min(512, H)
        xi = r % 2
        ai = r % 3
        sv, ksv = D_STG[r % 4]
        dma(p.sd_queue, xres[xi][:, 0:HH],
            x_hbm[xrow0:xrow0 + 128, HH * hh:HH * (hh + 1)],
            [], [("xres", xi)])
        ka = ("a", ai)
        p.op("act", [("acc", i, hh)], [ka],
             lambda e, i=i, hh=hh, ai=ai: e.activation(
                 out=abuf[ai][:, 0:HH], in_=acc[:, i, HH * hh:HH * (hh + 1)],
                 func=AF.Tanh, scale=0.5))
        p.op("dve", [ka, ("acc", i, hh)], [ksv],
             lambda e, i=i, hh=hh, ai=ai, sv=sv: e.scalar_tensor_tensor(
                 out=sv[:, 0:HH], in0=abuf[ai][:, 0:HH],
                 scalar=1.0, in1=acc[:, i, HH * hh:HH * (hh + 1)],
                 op0=ALU.add, op1=ALU.mult))
        ytk = [("ytok", i, g2) for g2 in range(G)
               if HH * hh <= 128 * g2 < HH * (hh + 1)]
        p.op("dve", [ksv] + ytk, [ksv],
             lambda e, i=i, hh=hh, sv=sv: e.scalar_tensor_tensor(
                 out=sv[:, 0:HH], in0=sv[:, 0:HH],
                 scalar=0.5, in1=ytok[:, i, HH * hh:HH * (hh + 1)],
                 op0=ALU.mult, op1=ALU.mult))
        kxr = ("xres", xi)
        p.op("dve", [ksv, kxr], [ksv],
             lambda e, sv=sv, xi=xi: e.tensor_tensor(
                 out=sv[:, 0:HH], in0=sv[:, 0:HH],
                 in1=xres[xi][:, 0:HH], op=ALU.add))
        p.op(p.gd_queue(), [ksv], [("outdma", u, i, hh)],
             lambda e, i=i, hh=hh, sv=sv, xrow0=xrow0: e.dma_start(
                 out=out_hbm[xrow0:xrow0 + 128, HH * hh:HH * (hh + 1)],
                 in_=sv[:, 0:HH]))

    from collections import deque
    pending = deque()

    witems = []
    for _u in range(UNITS):
        for e_i in range(E):
            for (it0, it1) in [(0, cfg.IH), (cfg.IH, KI)]:
                if it1 > it0:
                    for (col, bw) in gateup_blocks(it0, it1):
                        witems.append((_u, e_i, col, bw))
    wheel = WeightWheel(witems)

    def drain(k):
        for _ in range(k):
            if not pending:
                return
            t = pending.popleft()
            if t is not None:
                t()

    for u in range(UNITS):
        base = u * UT
        first_of_batch = (u % cfg.UPB) == 0
        import itertools
        if u == 0:
            for i in range(NT):
                for t in ln_steps_a(0, i, 128 * i):
                    t()
        for i0 in range(0, NT, 2):
            sa = ln_steps_b(u, i0, base + 128 * i0) + router_steps(u, i0)
            sb = (ln_steps_b(u, i0 + 1, base + 128 * (i0 + 1)) +
                  router_steps(u, i0 + 1)) if i0 + 1 < NT else []
            OFF = 2
            for k in range(len(sa) + OFF):
                if k < len(sa):
                    sa[k]()
                if 0 <= k - OFF < len(sb):
                    sb[k - OFF]()
            drain(2)
        if u > 0:
            for i in range(NT):
                pending.extend(d_tile_thunks(u - 1, i, (u - 1) * UT + 128 * i))
        for g in range(G):
            dt_slab(u, g)
            drain(2)
        tr_pending = None

        def pend_slab(g):
            nonlocal tr_pending
            it_th, tr_th = scan_slab_thunks(u, g, first_of_batch)
            pending.extend(it_th)
            if tr_pending:
                pending.extend(tr_pending)
            tr_pending = tr_th
        last_u = (u == UNITS - 1)
        for e_idx in range(E):
            if e_idx == min(5, E - 1) and u + 1 < UNITS:
                for i in range(NT):
                    pending.extend(ln_steps_a(u + 1, i,
                                              (u + 1) * UT + 128 * i))
            if last_u and e_idx == E - 1:
                if tr_pending:
                    pending.extend(tr_pending)
                    tr_pending = None
            elif e_idx < G:
                pend_slab(e_idx)
                if last_u and e_idx == E - 2:
                    for g in range(E - 1, G):
                        pend_slab(g)
            ffn_expert(u, e_idx, drain, wheel,
                       emit_combine=(last_u and e_idx == E - 1))
        for g in range(E, G):
            if not last_u:
                pend_slab(g)
        if tr_pending:
            pending.extend(tr_pending)
        drain(len(pending))

    finals = []
    for s2 in sorted(Prog.DMA_STREAMS):
        if p.cnt[s2]:
            finals.append((s2, p.cnt[s2]))
    with nc.Block() as block:
        p.emit_all(block, finals)
    stack.close()
    return nc



def _prep_weights(cfg, kw):
    c = np.ascontiguousarray
    H, I, G, N, KI = cfg.H, cfg.I, cfg.G, cfg.N, cfg.KI
    ER = cfg.E - 1

    def tile_h(a):
        a = np.asarray(a)
        rest = a.shape[1:]
        out = a.reshape(G, 128, *rest).transpose(1, 0, *range(2, a.ndim + 1))
        return c(out.reshape(128, -1))

    def tile_wgu(g, u):
        gt = g.T.reshape(G, 128, KI, 128)
        ut = u.T.reshape(G, 128, KI, 128)
        st = np.concatenate([gt, ut], axis=-1)
        return c(st.transpose(2, 1, 0, 3).reshape(KI * 128, G * 256))

    def tile_d(dmat):
        dT = dmat.T
        HH0 = min(512, H)
        IH = cfg.IH
        WDB = min(4, KI)
        parts = []
        for (it0, it1) in [(0, IH), (IH, KI)]:
            for hh in range(H // HH0):
                kt_ = it0
                while kt_ < it1:
                    bk = min(WDB, it1 - kt_)
                    blk = dT[128 * kt_:128 * (kt_ + bk),
                             HH0 * hh:HH0 * (hh + 1)]
                    parts.append(blk.reshape(bk, 128, HH0)
                                 .transpose(1, 0, 2).reshape(-1))
                    kt_ += bk
        return c(np.concatenate(parts))

    d = {
        "wdtT": tile_h(np.asarray(kw["W_dt"]).T),
        "bdt": tile_h(kw["b_dt"]),
        "alog": tile_h(kw["A_log"]),
        "bmat": tile_h(kw["B_mat"]),
        "cmat": tile_h(kw["C_mat"]),
        "dvec": tile_h(kw["D_vec"]),
        "wrT": tile_h(np.asarray(kw["wr"]).T),
        "rb": c(kw["rb"]),
        "wgu0": tile_wgu(kw["sg"], kw["su"]),
        "dT0": tile_d(kw["sd"]),
    }
    for e in range(ER):
        d[f"wgu{e + 1}"] = tile_wgu(kw["rg"][e], kw["ru"][e])
        d[f"dT{e + 1}"] = tile_d(kw["rd"][e])
    return d


def kernel(L0_attn_x, L0_attn_A_log, L0_attn_B_mat, L0_attn_C_mat,
           L0_attn_D_vec, L0_attn_W_dt, L0_attn_b_dt, L0_ffn_sg, L0_ffn_su,
           L0_ffn_sd, L0_ffn_rg, L0_ffn_ru, L0_ffn_rd, L0_ffn_wr, L0_ffn_rb):
    cfg = Cfg()
    n_cores = 8
    x = np.ascontiguousarray(np.asarray(L0_attn_x, dtype=np.float32))
    B, T, H = x.shape
    w = _prep_weights(cfg, {
        "W_dt": np.asarray(L0_attn_W_dt, np.float32),
        "b_dt": np.asarray(L0_attn_b_dt, np.float32),
        "A_log": np.asarray(L0_attn_A_log, np.float32),
        "B_mat": np.asarray(L0_attn_B_mat, np.float32),
        "C_mat": np.asarray(L0_attn_C_mat, np.float32),
        "D_vec": np.asarray(L0_attn_D_vec, np.float32),
        "wr": np.asarray(L0_ffn_wr, np.float32),
        "rb": np.asarray(L0_ffn_rb, np.float32),
        "sg": np.asarray(L0_ffn_sg, np.float32),
        "su": np.asarray(L0_ffn_su, np.float32),
        "sd": np.asarray(L0_ffn_sd, np.float32),
        "rg": np.asarray(L0_ffn_rg, np.float32),
        "ru": np.asarray(L0_ffn_ru, np.float32),
        "rd": np.asarray(L0_ffn_rd, np.float32),
    })
    nc = build_nc(cfg)
    in_maps = []
    for c in range(n_cores):
        xc = x[cfg.BPC * c: cfg.BPC * (c + 1)].reshape(cfg.BPC * T, H)
        m = {"x": np.ascontiguousarray(xc)}
        m.update(w)
        in_maps.append(m)
    res = run_bass_kernel_spmd(nc, in_maps, core_ids=list(range(n_cores)))
    out = np.concatenate(
        [r["out"].reshape(cfg.BPC, T, H) for r in res.results], axis=0)
    return out.astype(np.float32)
